# version 18
# baseline (speedup 1.0000x reference)
import numpy as np
import ml_dtypes
import concourse.bass as bass
import concourse.mybir as mybir
from concourse.bass_utils import run_bass_kernel_spmd

F32 = mybir.dt.float32
BF16 = mybir.dt.bfloat16
I32 = mybir.dt.int32
AF = mybir.ActivationFunctionType
ALU = mybir.AluOpType

NF4 = [-1.0, -0.6961928009986877, -0.5250730514526367, -0.39491748809814453,
       -0.28444138169288635, -0.18477343022823334, -0.09105003625154495, 0.0,
       0.07958029955625534, 0.16093020141124725, 0.24611230194568634,
       0.33791524171829224, 0.44070982933044434, 0.5626170039176941,
       0.7229568362236023, 1.0]

FULL_CFG = dict(D=4096, TC=1024, I=11008, NCORE=8, CPB=4, TQB=256, ICH=5,
                KVHEADS=8, DQQ=6)
EPS = 1e-5
LNB = 64


def _chunks(n, k):
    out, s = [], 0
    for j in range(k):
        ln = n // k + (1 if j < n % k else 0)
        out.append((s, ln))
        s += ln
    return out


class B:
    ENGINES = ("tensor", "vector", "scalar", "gpsimd", "sync")

    def __init__(self, nc, stack, simsafe=False):
        self.nc = nc
        self.stack = stack
        self.recs = {e: [] for e in self.ENGINES}
        self.sems = {}
        self.simsafe = simsafe
        self.phase = "init"

    def _sem(self, name):
        if name not in self.sems:
            h = self.stack.enter_context(self.nc.semaphore(name))
            self.sems[name] = [h, 0]
        return self.sems[name]

    def op(self, eng, fn, waits=(), inc=None, inc_by=1):
        ev = None
        incs = []
        if inc is not None:
            s = self._sem(inc)
            s[1] += inc_by if inc_by > 0 else 1
            ev = (s, s[1])
            incs.append((s, inc_by))
        kind = "cc" if inc_by == 0 else ("dma" if inc_by == 16 else "op")
        if inc is None and self.simsafe and fn is not None:
            s = self._sem("ord_" + eng)
            s[1] += 1
            ev = (s, s[1])
            incs.append((s, 1))
        self.recs[eng].append((fn, [w for w in waits if w is not None], incs,
                               kind, ev, self.phase))
        return ev

    def wait_only(self, eng, waits):
        self.recs[eng].append((None, [w for w in waits if w is not None], [],
                               "wait", None, self.phase))

    def dma(self, eng, out, in_, inc, waits=(), **kw):
        return self.op(eng, lambda e: e.dma_start(out=out, in_=in_, **kw),
                       waits=waits, inc=inc, inc_by=16)

    def cc(self, kind, aluop, groups, ins, outs, inc, waits=()):
        return self.op(
            "gpsimd",
            lambda e: e.collective_compute(kind, aluop, replica_groups=groups,
                                           ins=ins, outs=outs),
            waits=waits, inc=inc, inc_by=0)

    def emit(self, block):
        seen = {}

        def runner(eng_name):
            def body(e):
                prev_ev = None
                prev_cev = None
                for (fn, waits, incs, kind, ev, ph) in self.recs[eng_name]:
                    do_ord = (self.simsafe is True or
                              (isinstance(self.simsafe,
                                          (set, frozenset, list, tuple))
                               and (eng_name in self.simsafe or
                                    ph in self.simsafe or
                                    (ph + ":" + eng_name) in self.simsafe)))
                    comp_ord = (kind == "op" and
                                isinstance(self.simsafe,
                                           (set, frozenset, list, tuple)) and
                                (("compute" in self.simsafe and
                                  eng_name in ("vector", "scalar", "gpsimd"))
                                 or "simdbg" in self.simsafe))
                    allw = list(waits)
                    if do_ord and prev_ev is not None and fn is not None:
                        allw.append(prev_ev)
                    elif comp_ord and prev_cev is not None and fn is not None:
                        allw.append(prev_cev)
                    for (s, val) in allw:
                        key = (eng_name, id(s))
                        if seen.get(key, 0) >= val:
                            continue
                        e.wait_ge(s[0], val)
                        seen[key] = val
                    if fn is None:
                        continue
                    ins = fn(e)
                    for (s, by) in incs:
                        if by == 0:
                            ins.then_inc(s[0])
                        else:
                            ins.then_inc(s[0], by)
                    if ev is not None:
                        prev_ev = ev
                        if kind == "op":
                            prev_cev = ev
            return body
        block.tensor(runner("tensor"))
        block.vector(runner("vector"))
        block.scalar(runner("scalar"))
        block.gpsimd(runner("gpsimd"))
        block.sync(runner("sync"))


def build_program(cfg, simsafe=False):
    D, TC, I = cfg["D"], cfg["TC"], cfg["I"]
    NCORE, CPB, TQB, ICH = cfg["NCORE"], cfg["CPB"], cfg["TQB"], cfg["ICH"]
    HD = 128
    NH = D // HD
    NKV = cfg["KVHEADS"]
    GQ = NH // NKV
    KV = NKV * HD
    S = CPB * TC
    HB = D // 128
    HBH = HB // 2
    STT = S // 128
    IT = I // 128
    TBW = min(512, TC)
    NTB = TC // TBW
    NQS = TQB // 128
    NTQB = TC // TQB
    HSL = D // NCORE
    HBL = HSL // 128
    VW = NKV * (HD + 1)
    ich = _chunks(IT, ICH)
    maxit = max(ln for _, ln in ich)
    DQQ = cfg["DQQ"]
    kvgroups = [list(range(g * CPB, (g + 1) * CPB))
                for g in range(NCORE // CPB)]
    allg = [list(range(NCORE))]
    scale_qk = float(1.0 / np.sqrt(HD))
    n_lnb = TC // LNB

    nc = bass.Bass("TRN2", target_bir_lowering=False, debug=False)

    xT = nc.dram_tensor("xT", [D, TC], F32, kind="ExternalInput").ap()
    wqT = nc.dram_tensor("wqT", [D, D], F32, kind="ExternalInput").ap()
    wkT = nc.dram_tensor("wkT", [KV, D], F32, kind="ExternalInput").ap()
    wvT = nc.dram_tensor("wvT", [KV, D], F32, kind="ExternalInput").ap()
    woT = nc.dram_tensor("woT", [D, D], F32, kind="ExternalInput").ap()
    gc_ = nc.dram_tensor("gc", [HSL, I], I32, kind="ExternalInput").ap()
    gs_ = nc.dram_tensor("gs", [HSL, I], F32, kind="ExternalInput").ap()
    uc_ = nc.dram_tensor("uc", [HSL, I], I32, kind="ExternalInput").ap()
    us_ = nc.dram_tensor("us", [HSL, I], F32, kind="ExternalInput").ap()
    dc_ = nc.dram_tensor("dc", [I, HSL], I32, kind="ExternalInput").ap()
    ds_ = nc.dram_tensor("ds", [I, HSL], F32, kind="ExternalInput").ap()
    oT = nc.dram_tensor("oT", [D, TC], F32, kind="ExternalOutput").ap()

    qT_d = nc.dram_tensor("qT_d", [D, TC], BF16).ap()
    hoT_d = nc.dram_tensor("hoT_d", [D, TC], BF16).ap()
    part_d = nc.dram_tensor("part_d", [D, TC], F32).ap()
    kT_in = nc.dram_tensor("kT_in", [KV, TC], BF16).ap()
    kT_out = nc.dram_tensor("kT_out", [CPB * KV, TC], BF16).ap()
    v_in = nc.dram_tensor("v_in", [TC, VW], BF16).ap()
    v_out = nc.dram_tensor("v_out", [CPB * TC, VW], BF16).ap()
    qcols_g = [(it0 * 128, itn * 128) for (it0, itn) in _chunks(IT, cfg["DQQ"])]
    wg_in = [nc.dram_tensor("wg_in%d" % qi, [128, qn * HBL], BF16).ap()
             for qi, (q0, qn) in enumerate(qcols_g)]
    wg_out = [nc.dram_tensor("wg_out%d" % qi, [NCORE * 128, qn * HBL],
                             BF16).ap()
              for qi, (q0, qn) in enumerate(qcols_g)]
    wu_in = [nc.dram_tensor("wu_in%d" % qi, [128, qn * HBL], BF16).ap()
             for qi, (q0, qn) in enumerate(qcols_g)]
    wu_out = [nc.dram_tensor("wu_out%d" % qi, [NCORE * 128, qn * HBL],
                             BF16).ap()
              for qi, (q0, qn) in enumerate(qcols_g)]
    wd_in = [nc.dram_tensor("wd_in%d" % ci, [128, citn * HBL * 128],
                            BF16).ap()
             for ci, (cit0, citn) in enumerate(ich)]
    wd_out = [nc.dram_tensor("wd_out%d" % ci, [NCORE * 128,
                                               citn * HBL * 128], BF16).ap()
              for ci, (cit0, citn) in enumerate(ich)]

    from contextlib import ExitStack
    with ExitStack() as stack:
        e_ = stack.enter_context

        _qc = [(it0 * 128, itn * 128) for (it0, itn) in _chunks(IT, DQQ)]
        _gsz = max(1, (max(qn for _, qn in _qc)) // HSL)
        dqq = max(max(qn for _, qn in _qc), _gsz * HSL)
        NQSL = 8
        ySz = max(3 * dqq, 2 * maxit * 128 + 2 * HB * 128)
        zAttn = (2 * STT * 128 + 2 * STT * (HD + 1) + NQSL * TQB +
                 2 * STT * TQB + 4 * 128 + 6 * TQB)
        zProj = 2 * (HB * 128) + 2 * TBW + 2 * VW
        zFfn = maxit * TC + 8 * HBH * 128 + 2 * TBW
        zSz = max(zAttn, zProj, zFfn)
        nbf = HB * TC + ySz + zSz
        A = e_(nc.sbuf_tensor("A", [128, nbf], BF16))
        X0, Y0, Z0 = 0, HB * TC, HB * TC + ySz

        LNE = HB * LNB + 4 * LNB
        fSz = max(LNE + 2 * dqq, 8 * TBW) + 8
        Af = e_(nc.sbuf_tensor("Af", [128, fSz], F32))
        Ai = e_(nc.sbuf_tensor("Ai", [128, 2 * dqq], I32))

        ident = e_(nc.sbuf_tensor("ident", [128, 128], BF16))
        iota_t = e_(nc.sbuf_tensor("iota_t", [128, 128], F32))
        ones_col = e_(nc.sbuf_tensor("ones_col", [128, 1], F32))
        ones_row = e_(nc.sbuf_tensor("ones_row", [1, 128], F32))
        srow = e_(nc.sbuf_tensor("srow", [1, 6 * LNB], F32))
        rloc = e_(nc.sbuf_tensor("rloc", [128, 4], F32))
        P = [e_(nc.psum_tensor(f"ps{i}", [128, 512], F32)) for i in range(7)]
        Pb = e_(nc.psum_tensor("psb", [128, 1024], BF16))

        blk = e_(nc.Block())
        b = B(nc, stack, simsafe=simsafe)
        psfree = {}

        ev_iota = b.op("gpsimd", lambda e: e.iota(
            iota_t[:], pattern=[[1, 128]], base=0, channel_multiplier=-1,
            allow_small_or_imprecise_dtypes=True), inc="gq")
        ev_ident = b.op("vector", lambda e: e.tensor_scalar(
            ident[:], iota_t[:], 0.0, None, ALU.is_equal),
            waits=[ev_iota], inc="vq")
        b.op("vector", lambda e: e.memset(ones_col[:], 1.0), inc="vq")
        ev_ones = b.op("vector", lambda e: e.memset(ones_row[:], 1.0),
                       inc="vq")

        b.phase = "dq"
        cbf = A[:, Y0:Y0 + dqq]
        lut = A[:, Y0 + dqq:Y0 + 2 * dqq]
        dtmp = A[:, Y0 + 2 * dqq:Y0 + 3 * dqq]
        ci32 = [Ai[:, 0:dqq], Ai[:, dqq:2 * dqq]]
        LNE = HB * LNB + 4 * LNB
        scl2 = [Af[:, LNE:LNE + dqq], Af[:, LNE + dqq:LNE + 2 * dqq]]

        def dequant_lut(n, waits):
            ev = b.op("vector", lambda e, n=n: e.tensor_scalar(
                lut[:, :n], cbf[:, :n], 0.0, NF4[0], ALU.is_equal, ALU.mult),
                waits=waits, inc="vq")
            for v in range(1, 16):
                evt = b.op("vector", lambda e, v=v, n=n: e.tensor_scalar(
                    dtmp[:, :n], cbf[:, :n], float(v), NF4[v],
                    ALU.is_equal, ALU.mult), inc="vq")
                ev = b.op("vector", lambda e, n=n: e.tensor_add(
                    lut[:, :n], lut[:, :n], dtmp[:, :n]),
                    waits=[evt], inc="vq")
            return ev

        qcols = [(it0 * 128, itn * 128) for (it0, itn) in _chunks(IT, DQQ)]
        ev_cc_w = {}
        ev_scl_used = [None, None]
        ev_lut_used = None
        ev_cbf_used = None
        ev_ci_used = [None, None]
        dq_i = 0

        def dq_tile(load_fn, n, write_fn):
            nonlocal dq_i, ev_lut_used, ev_cbf_used
            islot = dq_i % 2
            ev_ld32 = load_fn(ci32[islot], "d_cb%d" % islot,
                              [ev_ci_used[islot]])
            sslot2 = dq_i % 2
            scl = scl2[sslot2]
            ev_s = load_fn(scl, "d_sc%d" % sslot2,
                           [ev_scl_used[sslot2]], scales=True)
            dq_i += 1
            ev_cast = b.op("vector", lambda e, n=n, islot=islot:
                           e.tensor_copy(cbf[:, :n], ci32[islot][:, :n]),
                           waits=[ev_ld32, ev_cbf_used], inc="vq")
            ev_ci_used[islot] = ev_cast
            ev_l = dequant_lut(n, [ev_cast, ev_lut_used])
            ev_cbf_used = ev_l
            ev_m = b.op("vector", lambda e, n=n, scl=scl: e.tensor_mul(
                lut[:, :n], lut[:, :n], scl[:, :n]),
                waits=[ev_l, ev_s], inc="vq")
            ev_scl_used[sslot2] = ev_m
            ev_w = write_fn([ev_m])
            ev_lut_used = ev_w
            return ev_w

        srcs = {"wg": (gc_, gs_), "wu": (uc_, us_), "wd": (dc_, ds_)}
        wins = {"wg": wg_in, "wu": wu_in, "wd": wd_in}
        wouts = {"wg": wg_out, "wu": wu_out, "wd": wd_out}
        qi_of_it = {}
        for qi, (q0, qn) in enumerate(qcols):
            for it in range(q0 // 128, (q0 + qn) // 128):
                qi_of_it[it] = qi
        gsz2 = max(1, dqq // HSL)
        wd_tile_its = []
        wd_tiles_per_ci = {}
        for ci, (cit0, citn) in enumerate(ich):
            s0w = cit0
            while s0w < cit0 + citn:
                itn = min(gsz2, cit0 + citn - s0w)
                wd_tile_its.append((ci, s0w, itn))
                wd_tiles_per_ci[ci] = wd_tiles_per_ci.get(ci, 0) + 1
                s0w += itn

        dq_worklist = []
        ag_waits = {}

        def _mk_gu_tile(wname, qi, hbl):
            (q0, qn) = qcols[qi]
            itnl = qn // 128
            codes, scarr = srcs[wname]
            win = wins[wname][qi]
            w4 = win.rearrange("p (it hbl i) -> p it hbl i",
                               it=itnl, hbl=HBL)

            def run():
                def loadf(dst, sem, waits, scales=False):
                    src = scarr if scales else codes
                    return b.dma(
                        "sync", dst[:, :qn],
                        src[hbl * 128:(hbl + 1) * 128, q0:q0 + qn],
                        inc=sem, waits=waits)

                def writef(waits):
                    return b.dma(
                        "gpsimd", w4[:, :, hbl, :],
                        lut[:, :qn].rearrange("p (it i) -> p it i",
                                              i=128),
                        inc="d_wr", waits=waits)
                ev_w = dq_tile(loadf, qn, writef)
                ag_waits.setdefault((wname, qi), []).append(ev_w)
                if len(ag_waits[(wname, qi)]) == HBL:
                    ev_cc_w[(wname, qi)] = b.cc(
                        "AllGather", ALU.bypass, allg,
                        [win], [wouts[wname][qi]],
                        inc="cc_%s%d" % (wname, qi),
                        waits=ag_waits[(wname, qi)])
            return run

        def _mk_wd_tile(ti):
            (ci, it0, itn) = wd_tile_its[ti]
            (cit0, citn) = ich[ci]
            n = itn * HSL
            codes, scarr = srcs["wd"]
            win = wins["wd"][ci]
            w4 = win.rearrange("p (it hbl i) -> p it hbl i",
                               it=citn, hbl=HBL)
            c3 = codes.rearrange("(it p) h -> p it h", p=128)
            s3 = scarr.rearrange("(it p) h -> p it h", p=128)

            def run():
                def loadf(dst, sem, waits, scales=False):
                    src = s3 if scales else c3
                    return b.dma(
                        "sync",
                        dst[:, :n].rearrange("p (it h) -> p it h",
                                             h=HSL),
                        src[:, it0:it0 + itn, :], inc=sem, waits=waits)

                def writef(waits):
                    lut4 = lut[:, :n].rearrange(
                        "p (it hbl i) -> p it hbl i", it=itn, hbl=HBL)
                    evw = None
                    for hbl in range(HBL):
                        evw = b.dma(
                            "gpsimd",
                            w4[:, it0 - cit0:it0 - cit0 + itn, hbl, :],
                            lut4[:, :, hbl, :],
                            inc="d_wr", waits=(waits if hbl == 0 else ()))
                    return evw
                ev_w = dq_tile(loadf, n, writef)
                ag_waits.setdefault(("wd", ci), []).append(ev_w)
                if len(ag_waits[("wd", ci)]) == wd_tiles_per_ci[ci]:
                    ev_cc_w[("wd", ci)] = b.cc(
                        "AllGather", ALU.bypass, allg,
                        [win], [wouts["wd"][ci]],
                        inc="cc_wd%d" % ci,
                        waits=ag_waits[("wd", ci)])
            return run

        for qi in range(len(qcols)):
            for wname in ("wg", "wu"):
                for hbl in range(HBL):
                    dq_worklist.append(_mk_gu_tile(wname, qi, hbl))
        for ti in range(len(wd_tile_its)):
            dq_worklist.append(_mk_wd_tile(ti))
        dq_next = [0]

        def emit_dq_tile(k=1):
            for _ in range(k):
                if dq_next[0] < len(dq_worklist):
                    dq_worklist[dq_next[0]]()
                    dq_next[0] += 1

        b.phase = "ln"
        xnT = A[:, X0:X0 + HB * TC]
        xnT3 = xnT.rearrange("p (hb t) -> p hb t", hb=HB)
        xblk = Af[:, 0:HB * LNB]
        xblk3 = xblk.rearrange("p (hb t) -> p hb t", hb=HB)
        xsq = [Af[:, HB * LNB + i * LNB: HB * LNB + (i + 1) * LNB]
               for i in range(2)]
        rstd_b = Af[:, HB * LNB + 2 * LNB: HB * LNB + 3 * LNB]
        mb_b = Af[:, HB * LNB + 3 * LNB: HB * LNB + 4 * LNB]
        xT3 = xT.rearrange("(hb p) t -> p hb t", p=128)

        ev_xblk_free = None
        ev_xn = None
        ev_xn_w = []
        for tb in range(n_lnb):
            t0 = tb * LNB
            ev_ld = b.dma("sync", xblk3, xT3[:, :, t0:t0 + LNB],
                          inc="d_x", waits=[ev_xblk_free])
            ev_sx = None
            for hb in range(HB):
                w0 = [ev_ld, ev_ones, psfree.get(4)] if hb == 0 else ()
                ev_sx = b.op("tensor", lambda e, hb=hb: e.matmul(
                    P[4][0:1, :LNB], ones_col[:], xblk3[:, hb, :],
                    start=(hb == 0), stop=(hb == HB - 1)),
                    waits=w0, inc="tq")
            ev_sq_mm = []
            for hb in range(HB):
                w0 = [ev_ld]
                if hb >= 2:
                    w0.append(ev_sq_mm[hb - 2])
                if hb < 2:
                    w0.append(psfree.get(5))
                evq = b.op("scalar", lambda e, hb=hb: e.activation(
                    xsq[hb % 2], xblk3[:, hb, :], AF.Square),
                    waits=w0, inc="sq")
                ev_sq_mm.append(b.op("tensor", lambda e, hb=hb: e.matmul(
                    P[5][0:1, :LNB], ones_col[:], xsq[hb % 2],
                    start=(hb == 0), stop=(hb == HB - 1)),
                    waits=[evq], inc="tq"))
            mean = srow[:, 0 * LNB:1 * LNB]
            ex2 = srow[:, 1 * LNB:2 * LNB]
            m2 = srow[:, 2 * LNB:3 * LNB]
            var = srow[:, 3 * LNB:4 * LNB]
            lnv = srow[:, 4 * LNB:5 * LNB]
            rstd = srow[:, 5 * LNB:6 * LNB]
            e1 = b.op("vector", lambda e: e.tensor_scalar(
                mean, P[4][0:1, :LNB], 1.0 / D, None, ALU.mult),
                waits=[ev_sx], inc="vq")
            e2 = b.op("vector", lambda e: e.tensor_scalar(
                ex2, P[5][0:1, :LNB], 1.0 / D, None, ALU.mult),
                waits=[ev_sq_mm[-1]], inc="vq")
            psfree[4], psfree[5] = e1, e2
            e3 = b.op("vector", lambda e: e.tensor_mul(m2, mean, mean),
                      waits=[e1], inc="vq")
            e4a = b.op("vector", lambda e: e.tensor_sub(var, ex2, m2),
                       waits=[e2, e3], inc="vq")
            e4 = b.op("vector", lambda e: e.tensor_scalar(
                var, var, EPS, None, ALU.add), waits=[e4a], inc="vq")
            e5 = b.op("scalar", lambda e: e.activation(
                lnv, var, AF.Ln), waits=[e4], inc="sq")
            e6 = b.op("scalar", lambda e: e.activation(
                rstd, lnv, AF.Exp, scale=-0.5), waits=[e5], inc="sq")
            e7 = b.op("vector", lambda e: e.scalar_tensor_tensor(
                m2, mean, -1.0, rstd, ALU.mult, ALU.mult),
                waits=[e6], inc="vq")
            e8 = b.op("tensor", lambda e: e.matmul(
                P[6][:, 0:LNB], ones_row[:], rstd, start=True, stop=True),
                waits=[e6, psfree.get(6)], inc="tq")
            e9 = b.op("tensor", lambda e: e.matmul(
                P[6][:, LNB:2 * LNB], ones_row[:], m2, start=True, stop=True),
                waits=[e7], inc="tq")
            eb1 = b.op("vector", lambda e: e.tensor_copy(
                rstd_b, P[6][:, 0:LNB]), waits=[e8, e9], inc="vq")
            eb2 = b.op("vector", lambda e: e.tensor_copy(
                mb_b, P[6][:, LNB:2 * LNB]), waits=[e9], inc="vq")
            psfree[6] = eb2
            for hb in range(HB):
                evm = b.op("vector", lambda e, hb=hb: e.tensor_mul(
                    xsq[hb % 2], xblk3[:, hb, :], rstd_b),
                    waits=[eb1, eb2], inc="vq")
                ev_xn = b.op("vector", lambda e, hb=hb, t0=t0: e.tensor_add(
                    xnT3[:, hb, t0:t0 + LNB], xsq[hb % 2], mb_b),
                    waits=[evm], inc="vq")
            ev_xblk_free = ev_xn
            ev_xn_w.append(ev_xn)
            if tb % 2 == 1:
                emit_dq_tile(1)

        b.phase = "proj"
        wpan = [A[:, Z0 + i * HB * 128: Z0 + (i + 1) * HB * 128]
                for i in range(2)]
        ps0 = Z0 + 2 * HB * 128
        pstg = [A[:, ps0 + i * TBW: ps0 + (i + 1) * TBW] for i in range(2)]
        vs0 = ps0 + 2 * TBW
        vstg = [A[:, vs0 + i * VW: vs0 + (i + 1) * VW] for i in range(2)]

        ev_vst_used = []
        for i in range(2):
            ev_vst_used.append(b.op(
                "vector", lambda e, i=i: e.memset(
                    vstg[i].rearrange("p (h c) -> p h c", c=HD + 1)[:, :, HD:],
                    1.0), inc="vq"))

        def cast_panel(slot, wT, dt, waits, sem):
            return b.dma("gpsimd", wpan[slot][:, :HB * 128],
                         wT[dt * 128:(dt + 1) * 128, :],
                         inc=sem, waits=waits)

        ev_pan_used = [None, None]
        ev_pstg_used = [None, None]
        kv_evs = []
        q_evs = []
        pidx = 0

        WLN = TBW // LNB
        for dt in range(KV // 128):
            slot = pidx % 2
            ev_pan = cast_panel(slot, wkT, dt, [ev_pan_used[slot]],
                                "d_wp%d" % slot)
            ev_mm = None
            for tb in range(NTB):
                bank = (pidx * NTB + tb) % 4
                for hb in range(HB):
                    w0 = ([ev_pan, ev_xn_w[(tb + 1) * WLN - 1],
                           psfree.get(bank)]
                          if hb == 0 else ())
                    ev_mm = b.op("tensor", lambda e, hb=hb, tb=tb, bank=bank, slot=slot: e.matmul(
                        P[bank][:, :TBW],
                        wpan[slot].rearrange("p (hb d) -> p hb d", hb=HB)[:, hb, :128],
                        xnT3[:, hb, tb * TBW:(tb + 1) * TBW],
                        start=(hb == 0), stop=(hb == HB - 1)),
                        waits=w0, inc="tq")
                sslot = bank % 2
                ev_ev = b.op("scalar", lambda e, bank=bank, sslot=sslot: e.activation(
                    pstg[sslot][:, :TBW], P[bank][:, :TBW], AF.Copy),
                    waits=[ev_mm, ev_pstg_used[sslot]], inc="sq")
                psfree[bank] = ev_ev
                ev_st = b.dma("scalar",
                              kT_in[dt * 128:(dt + 1) * 128,
                                    tb * TBW:(tb + 1) * TBW],
                              pstg[sslot][:, :TBW], inc="d_ps%d" % sslot,
                              waits=[ev_ev])
                ev_pstg_used[sslot] = ev_st
                kv_evs.append(ev_st)
            ev_pan_used[slot] = ev_mm
            pidx += 1
        nwv = 128
        for tt in range(TC // 128):
            for dh in range(KV // nwv):
                slot = pidx % 2
                ev_pan = cast_panel(slot, wvT, dh, [ev_pan_used[slot]],
                                    "d_wp%d" % slot)
                bank = pidx % 4
                ev_mm = None
                for hb in range(HB):
                    w0 = ([ev_pan, ev_xn_w[(tt + 1) * (128 // LNB) - 1],
                           psfree.get(bank)]
                          if hb == 0 else ())
                    ev_mm = b.op("tensor", lambda e, hb=hb, tt=tt, bank=bank, slot=slot, nwv=nwv: e.matmul(
                        P[bank][:, :nwv],
                        xnT3[:, hb, tt * 128:(tt + 1) * 128],
                        wpan[slot].rearrange("p (hb d) -> p hb d", hb=HB)[:, hb, :nwv],
                        start=(hb == 0), stop=(hb == HB - 1)),
                        waits=w0, inc="tq")
                ev_pan_used[slot] = ev_mm
                vslot = pidx % 2
                nevac = nwv // HD
                ev_ev = None
                for j in range(nevac):
                    w0 = [ev_mm, ev_vst_used[vslot]] if j == 0 else ()
                    ev_ev = b.op("scalar", lambda e, j=j, bank=bank, vslot=vslot: e.activation(
                        vstg[vslot].rearrange("p (h c) -> p h c",
                                              c=HD + 1)[:, j, :HD],
                        P[bank][:, j * HD:(j + 1) * HD], AF.Copy),
                        waits=w0, inc="sq")
                psfree[bank] = ev_ev
                ev_st = b.dma(
                    "scalar",
                    v_in[tt * 128:(tt + 1) * 128,
                         dh * nevac * (HD + 1):(dh + 1) * nevac * (HD + 1)],
                    vstg[vslot][:, :nevac * (HD + 1)],
                    inc="d_vs%d" % vslot, waits=[ev_ev])
                ev_vst_used[vslot] = ev_st
                kv_evs.append(ev_st)
                pidx += 1
        ev_cc_k = b.cc("AllGather", ALU.bypass, kvgroups, [kT_in], [kT_out],
                       inc="cck", waits=kv_evs)
        ev_cc_v = b.cc("AllGather", ALU.bypass, kvgroups, [v_in], [v_out],
                       inc="ccv", waits=kv_evs)
        for dt in range(D // 128):
            slot = pidx % 2
            ev_pan = cast_panel(slot, wqT, dt, [ev_pan_used[slot]],
                                "d_wp%d" % slot)
            ev_mm = None
            for tb in range(NTB):
                bank = (pidx * NTB + tb) % 4
                for hb in range(HB):
                    w0 = ([ev_pan, ev_xn_w[(tb + 1) * WLN - 1],
                           psfree.get(bank)]
                          if hb == 0 else ())
                    ev_mm = b.op("tensor", lambda e, hb=hb, tb=tb, bank=bank, slot=slot: e.matmul(
                        P[bank][:, :TBW],
                        wpan[slot].rearrange("p (hb d) -> p hb d", hb=HB)[:, hb, :128],
                        xnT3[:, hb, tb * TBW:(tb + 1) * TBW],
                        start=(hb == 0), stop=(hb == HB - 1)),
                        waits=w0, inc="tq")
                sslot = bank % 2
                ev_ev = b.op("scalar", lambda e, bank=bank, sslot=sslot: e.activation(
                    pstg[sslot][:, :TBW], P[bank][:, :TBW], AF.Copy),
                    waits=[ev_mm, ev_pstg_used[sslot]], inc="sq")
                psfree[bank] = ev_ev
                ev_st = b.dma("scalar",
                              qT_d[dt * 128:(dt + 1) * 128,
                                   tb * TBW:(tb + 1) * TBW],
                              pstg[sslot][:, :TBW], inc="d_ps%d" % sslot,
                              waits=[ev_ev])
                ev_pstg_used[sslot] = ev_st
                q_evs.append(ev_st)
            ev_pan_used[slot] = ev_mm
            pidx += 1

        emit_dq_tile(len(dq_worklist))
        ev_dq_done = ev_lut_used

        b.phase = "attn"
        za = Z0
        kTs = [A[:, za + i * STT * 128: za + (i + 1) * STT * 128]
               for i in range(2)]
        za += 2 * STT * 128
        Vs = [A[:, za + i * STT * (HD + 1): za + (i + 1) * STT * (HD + 1)]
              for i in range(2)]
        za += 2 * STT * (HD + 1)
        qTs = [A[:, za + i * TQB: za + (i + 1) * TQB] for i in range(NQSL)]
        za += NQSL * TQB
        expP = [A[:, za + i * STT * TQB: za + (i + 1) * STT * TQB]
                for i in range(2)]
        za += 2 * STT * TQB
        hosc = [A[:, za + i * 128: za + (i + 1) * 128] for i in range(4)]
        za += 4 * 128
        hostg = [A[:, za + i * TQB: za + (i + 1) * TQB] for i in range(6)]

        kT_o3 = kT_out.rearrange("(r d) t -> d r t", r=CPB)
        v_o3 = v_out.rearrange("(tt p) c -> p tt c", p=128)

        ev_att_by_u = []
        ev_sc_last = {}
        ev_q_used = [None] * NQSL
        ev_hosc_used = [None, None, None, None]
        ev_hostg_used = [None] * 6
        ev_trev_prev = [None, None]
        ho_evs = []
        pend = None
        u = 0

        def emit_epilogue(st, ev_avs):
            nonlocal ev_trev_prev
            (qh, tqb, es, ks, ev_exp_u, ev_lv, uu) = st
            avbs = [4, 5] if uu % 2 == 0 else [3, 6]
            ev_tr = None
            for tqs in range(NQS):
                avb = avbs[tqs % 2]
                el = b.op("scalar", lambda e, tqs=tqs, avb=avb: e.activation(
                    rloc[:, tqs:tqs + 1], P[avb][:, HD:HD + 1],
                    AF.Ln), waits=[ev_avs[tqs]], inc="sq")
                er = b.op("scalar", lambda e, tqs=tqs: e.activation(
                    rloc[:, 2 + tqs:3 + tqs], rloc[:, tqs:tqs + 1],
                    AF.Exp, scale=-1.0), waits=[el], inc="sq")
                hs = (uu % 2) * 2 + (tqs % 2)
                eh = b.op("scalar", lambda e, tqs=tqs, avb=avb, hs=hs: e.activation(
                    hosc[hs], P[avb][:, :HD], AF.Copy,
                    scale=rloc[:, 2 + tqs:3 + tqs]),
                    waits=[er, ev_hosc_used[hs]], inc="sq")
                psfree[avb] = eh
                pbh = (uu % 2) * NQS * 128
                ev_tr = b.op("tensor", lambda e, tqs=tqs, hs=hs, pbh=pbh: e.transpose(
                    Pb[:, pbh + tqs * 128:pbh + (tqs + 1) * 128], hosc[hs],
                    ident[:]),
                    waits=[eh, ev_ident, ev_trev_prev[uu % 2]], inc="tq")
                ev_hosc_used[hs] = ev_tr
            ev_att_by_u.append(ev_avs[-1])
            hss = len(ev_att_by_u) % 6
            pbh = (uu % 2) * NQS * 128
            ev_he = b.op("scalar", lambda e, hss=hss, pbh=pbh: e.activation(
                hostg[hss][:, :TQB], Pb[:, pbh:pbh + TQB], AF.Copy),
                waits=[ev_tr, ev_hostg_used[hss]], inc="sq")
            ev_trev_prev[uu % 2] = ev_he
            ev_hst = b.dma(
                "scalar",
                hoT_d[qh * 128:(qh + 1) * 128,
                      tqb * TQB:(tqb + 1) * TQB],
                hostg[hss][:, :TQB], inc="d_h%d" % hss,
                waits=[ev_he])
            ev_hostg_used[hss] = ev_hst
            ho_evs.append(ev_hst)

        def emit_av_pair(st, pair, ev_avs):
            (qh, tqb, es, ks, ev_exp_u, ev_lv, uu) = st
            avbs = [4, 5] if uu % 2 == 0 else [3, 6]
            for tqs in range(NQS):
                avb = avbs[tqs % 2]
                for tkt in (2 * pair, 2 * pair + 1):
                    w0 = []
                    if tqs == 0:
                        w0.append(ev_exp_u[pair])
                    if tkt == 0:
                        w0 += [ev_lv, psfree.get(avb)]
                    ev_avs[tqs] = b.op("tensor", lambda e, tkt=tkt, tqs=tqs, avb=avb, es=es, ks=ks: e.matmul(
                        P[avb][:, :HD + 1],
                        expP[es][:, tkt * TQB + tqs * 128:
                                 tkt * TQB + tqs * 128 + 128],
                        Vs[ks].rearrange("p (tt c) -> p tt c",
                                         c=HD + 1)[:, tkt, :],
                        start=(tkt == 0), stop=(tkt == STT - 1),
                        skip_group_check=True),
                        waits=w0, inc="tq")

        units = [(kvh, g, tqb, kvh * GQ + g)
                 for kvh in range(NKV) for g in range(GQ)
                 for tqb in range(NTQB)]

        ev_lk = {}
        ev_lv = {}
        ev_lq_by_u = {}

        def load_kv(kvh):
            ks = kvh % 2
            ev_lk[kvh] = b.dma(
                "scalar", kTs[ks].rearrange("p (r t) -> p r t", r=CPB),
                kT_o3[kvh * 128:(kvh + 1) * 128, :, :],
                inc="d_k%d" % ks,
                waits=[ev_cc_k, ev_sc_last.get(kvh - 2)] + q_evs)
            wv_v = [ev_cc_v] + q_evs
            if kvh >= 2:
                wv_v.append(ev_att_by_u[(kvh - 1) * GQ * NTQB - 1])
            ev_lv[kvh] = b.dma(
                "scalar", Vs[ks].rearrange("p (tt c) -> p tt c", c=HD + 1),
                v_o3[:, :, kvh * (HD + 1):(kvh + 1) * (HD + 1)],
                inc="d_v%d" % ks, waits=wv_v)

        def load_q(uu):
            qh = units[uu][3]
            tqb = units[uu][2]
            qs = uu % NQSL
            ev_lq_by_u[uu] = b.dma(
                "scalar", qTs[qs],
                qT_d[qh * 128:(qh + 1) * 128, tqb * TQB:(tqb + 1) * TQB],
                inc="d_q%d" % qs,
                waits=[ev_q_used[qs]] + q_evs)

        load_kv(0)
        for w in range(min(NQSL, len(units))):
            load_q(w)
        for u, (kvh, g, tqb, qh) in enumerate(units):
            ks = kvh % 2
            qs = u % NQSL
            es = u % 2
            if g == 0 and tqb == min(1, NTQB - 1) and kvh + 1 < NKV:
                load_kv(kvh + 1)
            ev_exp_u = []
            ev_avs = [None] * NQS
            ev_mm = None
            for tkt in range(STT):
                pair, half = tkt // 2, tkt % 2
                bank = pair % 3
                w0 = [ev_lk[kvh], ev_lq_by_u[u]]
                if half == 0:
                    w0.append(psfree.get(bank))
                ev_mm = b.op("tensor", lambda e, tkt=tkt, bank=bank, half=half, ks=ks, qs=qs: e.matmul(
                    P[bank][:, half * TQB:(half + 1) * TQB],
                    kTs[ks][:, tkt * 128:(tkt + 1) * 128],
                    qTs[qs][:, :TQB], start=True, stop=True),
                    waits=w0, inc="tq")
                ev_sc_last[kvh] = ev_mm
                if half == 1:
                    eex = b.op("scalar", lambda e, pair=pair, bank=bank, es=es: e.activation(
                        expP[es][:, 2 * pair * TQB:2 * (pair + 1) * TQB],
                        P[bank][:, :2 * TQB], AF.Exp, scale=scale_qk),
                        waits=[ev_mm], inc="sq")
                    ev_exp_u.append(eex)
                    psfree[bank] = eex
                    if pend is not None:
                        emit_av_pair(pend[0], pair, pend[1])
            ev_q_used[qs] = ev_mm
            if u + NQSL < len(units):
                load_q(u + NQSL)
            if pend is not None:
                emit_epilogue(pend[0], pend[1])
            pend = ((qh, tqb, es, ks, ev_exp_u, ev_lv[kvh], u), ev_avs)
        for pair in range(STT // 2):
            emit_av_pair(pend[0], pair, pend[1])
        emit_epilogue(pend[0], pend[1])

        b.phase = "ffn"
        zf = Z0
        intT = A[:, zf:zf + maxit * TC]
        zf += maxit * TC
        gpan = [A[:, zf + i * HBH * 128: zf + (i + 1) * HBH * 128]
                for i in range(8)]
        zf += 8 * HBH * 128
        sg = [A[:, zf + i * TBW: zf + (i + 1) * TBW] for i in range(2)]
        wdp = [A[:, Y0 + i * maxit * 128: Y0 + (i + 1) * maxit * 128]
               for i in range(2)]
        wop = [A[:, Y0 + 2 * maxit * 128 + i * HB * 128:
                 Y0 + 2 * maxit * 128 + (i + 1) * HB * 128]
               for i in range(2)]
        hoT_sb = A[:, X0:X0 + HB * TC]
        hoT_sb3 = hoT_sb.rearrange("p (hb t) -> p hb t", hb=HB)
        pastg = [Af[:, i * TBW:(i + 1) * TBW] for i in range(2)]
        outstg = [Af[:, (2 + i) * TBW:(3 + i) * TBW] for i in range(2)]
        xr = [Af[:, (4 + i) * TBW:(5 + i) * TBW] for i in range(4)]

        wg_o = [w.rearrange("(r p) (it q) -> p it r q", p=128,
                            it=qn // 128)
                for w, (q0, qn) in zip(wg_out, qcols)]
        wu_o = [w.rearrange("(r p) (it q) -> p it r q", p=128,
                            it=qn // 128)
                for w, (q0, qn) in zip(wu_out, qcols)]
        wd_o = [w.rearrange("(r p) (it hbl i) -> p r it hbl i",
                            p=128, it=citn, hbl=HBL)
                for w, (cit0, citn) in zip(wd_out, ich)]



        ev_gpan_used = [None] * 8
        ev_sg_used = [None, None]
        ev_int_rd = None
        ev_wdp_used = [None, None]
        ev_wop_used = [None, None]
        ev_stg_used = [None, None]
        ev_xr_used = [None] * 4
        ev_xr_ld = {}
        ev_pa_st = {}
        out_evs = []
        ffn_first = [ev_xn]
        dn_pair = [(4, 5), (0, 1)]

        def load_gu(kind, it, itl):
            evs = []
            qi = qi_of_it[it]
            itloc = it - qcols[qi][0] // 128
            src = (wg_o if kind == 0 else wu_o)[qi]
            ccw = ev_cc_w[("wg" if kind == 0 else "wu", qi)]
            for half in range(2):
                slot = (itl % 2) * 4 + kind * 2 + half
                rh = NCORE // 2
                ev = b.dma(
                    "scalar",
                    gpan[slot].rearrange("p (r q) -> p r q", r=rh),
                    src[:, itloc, half * rh:(half + 1) * rh, :],
                    inc="d_gp%d" % slot,
                    waits=[ccw, ev_gpan_used[slot]] + ho_evs)
                evs.append(ev)
            return evs

        def load_dn(ci, ht, cit0, citn, last_chunk):
            wslot = ht % 2
            ev_wd = b.dma("sync",
                          wdp[wslot][:, :citn * 128].rearrange(
                              "p (it i) -> p it i", i=128),
                          wd_o[ci][:, ht // HBL, :, ht % HBL, :],
                          inc="d_wdl%d" % wslot,
                          waits=[ev_cc_w[("wd", ci)], ev_dq_done,
                                 ev_wdp_used[wslot]])
            ev_wo = None
            if last_chunk:
                ev_wo = b.dma("gpsimd",
                              wop[wslot][:, :HB * 128],
                              woT[ht * 128:(ht + 1) * 128, :],
                              inc="d_wo%d" % wslot,
                              waits=[ev_wop_used[wslot], ev_dq_done])
            for tb in range(NTB):
                xslot = (ht % 2) * 2 + tb
                if ci == 0:
                    src = xT[ht * 128:(ht + 1) * 128,
                             tb * TBW:(tb + 1) * TBW]
                    w0 = [ev_xr_used[xslot], ev_dq_done]
                else:
                    src = part_d[ht * 128:(ht + 1) * 128,
                                 tb * TBW:(tb + 1) * TBW]
                    w0 = [ev_xr_used[xslot]] + ev_pa_st[(ci - 1, ht)]
                ev_xr_ld[(ht, tb)] = b.dma("sync", xr[xslot], src,
                                           inc="d_xr%d" % xslot, waits=w0)
            return ev_wd, ev_wo

        for ci, (cit0, citn) in enumerate(ich):
            last_chunk = (ci == ICH - 1)
            ev_dve_int = None
            gate_mm = [None] * NTB
            up_mm = [None] * NTB
            for itl in range(citn):
                it = cit0 + itl
                pg = load_gu(0, it, itl)
                pu = load_gu(1, it, itl)
                for kind in range(2):
                    pan = pg if kind == 0 else pu
                    for tb in range(NTB):
                        bank = kind * 2 + (tb % 2)
                        ev_mm = None
                        for hb in range(HB):
                            half, hloc = divmod(hb, HBH)
                            slot = (itl % 2) * 4 + kind * 2 + half
                            w0 = ()
                            if hb == 0:
                                w0 = [pan[0], psfree.get(bank)] + ffn_first
                                if itl == 0 and tb == 0 and kind == 0:
                                    w0 = w0 + [ev_int_rd]
                            elif hb == HBH:
                                w0 = [pan[1]]
                            ev_mm = b.op(
                                "tensor",
                                lambda e, hb=hb, hloc=hloc, slot=slot,
                                bank=bank, tb=tb: e.matmul(
                                    P[bank][:, :TBW],
                                    gpan[slot].rearrange(
                                        "p (hb i) -> p hb i",
                                        hb=HBH)[:, hloc, :],
                                    xnT3[:, hb, tb * TBW:(tb + 1) * TBW],
                                    start=(hb == 0), stop=(hb == HB - 1)),
                                waits=w0, inc="tq")
                        ffn_first = []
                        if kind == 0:
                            gate_mm[tb] = ev_mm
                        else:
                            up_mm[tb] = ev_mm
                for j in range(4):
                    ev_gpan_used[(itl % 2) * 4 + j] = up_mm[NTB - 1]
                for tb in range(NTB):
                    sslot = (itl * NTB + tb) % 2
                    ev_si = b.op("scalar", lambda e, tb=tb, sslot=sslot:
                                 e.activation(sg[sslot][:, :TBW],
                                              P[tb % 2][:, :TBW], AF.Silu),
                                 waits=[gate_mm[tb], ev_sg_used[sslot]],
                                 inc="sq")
                    psfree[tb % 2] = ev_si
                    ev_dve_int = b.op(
                        "vector", lambda e, itl=itl, tb=tb, sslot=sslot:
                        e.tensor_mul(
                            intT[:, itl * TC + tb * TBW:
                                 itl * TC + (tb + 1) * TBW],
                            sg[sslot][:, :TBW], P[2 + (tb % 2)][:, :TBW]),
                        waits=[ev_si, up_mm[tb]], inc="vq")
                    ev_sg_used[sslot] = ev_dve_int
                    psfree[2 + (tb % 2)] = ev_dve_int
            ev_gu_done = up_mm[NTB - 1]
            ev_hoT_ld = None
            if last_chunk:
                ev_hoT_ld = b.dma("sync", hoT_sb3,
                                  hoT_d.rearrange("(hb p) t -> p hb t",
                                                  p=128),
                                  inc="d_hol",
                                  waits=[ev_gu_done] + ho_evs)
            ev_wd, ev_wo = load_dn(ci, 0, cit0, citn, last_chunk)
            for ht in range(HB):
                wslot = ht % 2
                ev_wd_cur, ev_wo_cur = ev_wd, ev_wo
                if ht + 1 < HB:
                    ev_wd, ev_wo = load_dn(ci, ht + 1, cit0, citn,
                                           last_chunk)
                for tb in range(NTB):
                    bank = dn_pair[ht % 2][tb]
                    ev_mm = None
                    for itl in range(citn):
                        w0 = ()
                        if itl == 0:
                            w0 = [ev_wd_cur, psfree.get(bank), ev_dve_int]
                        ev_mm = b.op(
                            "tensor",
                            lambda e, itl=itl, tb=tb, bank=bank, wslot=wslot,
                            citn=citn, last_chunk=last_chunk: e.matmul(
                                P[bank][:, :TBW],
                                wdp[wslot][:, itl * 128:(itl + 1) * 128],
                                intT[:, itl * TC + tb * TBW:
                                     itl * TC + (tb + 1) * TBW],
                                start=(itl == 0),
                                stop=(itl == citn - 1 and not last_chunk)),
                            waits=w0, inc="tq")
                    if last_chunk:
                        for hb in range(HB):
                            w0 = [ev_wo_cur, ev_hoT_ld] if hb == 0 else ()
                            ev_mm = b.op(
                                "tensor",
                                lambda e, hb=hb, tb=tb, bank=bank,
                                wslot=wslot: e.matmul(
                                    P[bank][:, :TBW],
                                    wop[wslot].rearrange(
                                        "p (hb i) -> p hb i",
                                        hb=HB)[:, hb, :],
                                    hoT_sb3[:, hb,
                                            tb * TBW:(tb + 1) * TBW],
                                    start=False, stop=(hb == HB - 1)),
                                waits=w0, inc="tq")
                    ev_int_rd = ev_mm
                    ev_wdp_used[wslot] = ev_mm
                    if last_chunk:
                        ev_wop_used[wslot] = ev_mm
                    stg = tb % 2
                    xslot = (ht % 2) * 2 + tb
                    dststg = outstg[stg] if last_chunk else pastg[stg]
                    ev_e = b.op(
                        "vector",
                        lambda e, bank=bank, dststg=dststg, xslot=xslot:
                        e.tensor_add(dststg, P[bank][:, :TBW], xr[xslot]),
                        waits=[ev_mm, ev_xr_ld[(ht, tb)],
                               ev_stg_used[stg]], inc="vq")
                    psfree[bank] = ev_e
                    ev_xr_used[xslot] = ev_e
                    dst = (oT if last_chunk else part_d)[
                        ht * 128:(ht + 1) * 128, tb * TBW:(tb + 1) * TBW]
                    ev_s = b.dma("sync", dst, dststg,
                                 inc=("d_out%d" % stg if last_chunk
                                      else "d_pa%d" % stg),
                                 waits=[ev_e])
                    ev_stg_used[stg] = ev_s
                    if last_chunk:
                        out_evs.append(ev_s)
                    else:
                        ev_pa_st.setdefault((ci, ht), []).append(ev_s)

        b.wait_only("sync", out_evs)
        b.emit(blk)
    return nc


def _prep_inputs(x, Wq, Wk, Wv, Wo, gcodes, gscales, ucodes, uscales,
                 dcodes, dscales, cfg):
    D, TC, NCORE = cfg["D"], cfg["TC"], cfg["NCORE"]
    HSL = D // NCORE
    BLK = 64
    S_all = x.reshape(-1, D)
    xTf = np.ascontiguousarray(S_all.T.astype(np.float32))

    def _pl(wT):
        Kd, Md = wT.shape
        return np.ascontiguousarray(
            wT.reshape(Kd // 128, 128, Md // 128, 128)
            .transpose(2, 1, 0, 3).reshape(Md, Kd))
    wqT = _pl(Wq.T)
    wkT = _pl(Wk.T)
    wvT = _pl(Wv.T)
    woT = _pl(Wo.T)
    gcT = np.ascontiguousarray(gcodes.T)
    ucT = np.ascontiguousarray(ucodes.T)
    gsx = np.repeat(np.ascontiguousarray(gscales.T), BLK, axis=0)
    usx = np.repeat(np.ascontiguousarray(uscales.T), BLK, axis=0)
    dcT = np.ascontiguousarray(dcodes.T)
    maps = []
    for c in range(NCORE):
        dsx_c = np.repeat(
            np.ascontiguousarray(dscales.T[:, c * HSL:(c + 1) * HSL]),
            BLK, axis=0)
        maps.append({
            "xT": np.ascontiguousarray(xTf[:, c * TC:(c + 1) * TC]),
            "wqT": wqT, "wkT": wkT, "wvT": wvT, "woT": woT,
            "gc": np.ascontiguousarray(gcT[c * HSL:(c + 1) * HSL]),
            "gs": np.ascontiguousarray(gsx[c * HSL:(c + 1) * HSL]),
            "uc": np.ascontiguousarray(ucT[c * HSL:(c + 1) * HSL]),
            "us": np.ascontiguousarray(usx[c * HSL:(c + 1) * HSL]),
            "dc": np.ascontiguousarray(dcT[:, c * HSL:(c + 1) * HSL]),
            "ds": np.ascontiguousarray(dsx_c),
        })
    return maps


def run_cfg(inputs_dict, cfg, simsafe=False):
    maps = _prep_inputs(cfg=cfg, **inputs_dict)
    nc = build_program(cfg, simsafe=simsafe)
    res = run_bass_kernel_spmd(nc, maps, core_ids=list(range(cfg["NCORE"])))
    outs = [np.asarray(r["oT"], np.float32) for r in res.results]
    full = np.concatenate(outs, axis=1)
    return np.ascontiguousarray(full.T)


def kernel(L0_attn_x, L0_attn_Wq, L0_attn_Wk, L0_attn_Wv, L0_attn_Wo,
           L0_ffn_gate_codes, L0_ffn_gate_scales, L0_ffn_up_codes,
           L0_ffn_up_scales, L0_ffn_down_codes, L0_ffn_down_scales):
    cfg = FULL_CFG
    x = np.asarray(L0_attn_x, np.float32)
    B_, T_, D_ = x.shape
    out = run_cfg(cfg=cfg, simsafe=False, inputs_dict=dict(
        x=x,
        Wq=np.asarray(L0_attn_Wq, np.float32),
        Wk=np.asarray(L0_attn_Wk, np.float32),
        Wv=np.asarray(L0_attn_Wv, np.float32),
        Wo=np.asarray(L0_attn_Wo, np.float32),
        gcodes=np.asarray(L0_ffn_gate_codes, np.int32),
        gscales=np.asarray(L0_ffn_gate_scales, np.float32),
        ucodes=np.asarray(L0_ffn_up_codes, np.int32),
        uscales=np.asarray(L0_ffn_up_scales, np.float32),
        dcodes=np.asarray(L0_ffn_down_codes, np.int32),
        dscales=np.asarray(L0_ffn_down_scales, np.float32),
    ))
    return out.reshape(B_, T_, D_)


# version 19
# speedup vs baseline: 1.0330x; 1.0330x over previous
import numpy as np
import ml_dtypes
import concourse.bass as bass
import concourse.mybir as mybir
from concourse.bass_utils import run_bass_kernel_spmd

F32 = mybir.dt.float32
BF16 = mybir.dt.bfloat16
I32 = mybir.dt.int32
AF = mybir.ActivationFunctionType
ALU = mybir.AluOpType

NF4 = [-1.0, -0.6961928009986877, -0.5250730514526367, -0.39491748809814453,
       -0.28444138169288635, -0.18477343022823334, -0.09105003625154495, 0.0,
       0.07958029955625534, 0.16093020141124725, 0.24611230194568634,
       0.33791524171829224, 0.44070982933044434, 0.5626170039176941,
       0.7229568362236023, 1.0]

FULL_CFG = dict(D=4096, TC=1024, I=11008, NCORE=8, CPB=4, TQB=256, ICH=5,
                KVHEADS=8, DQQ=6)
EPS = 1e-5
LNB = 64


def _chunks(n, k):
    out, s = [], 0
    for j in range(k):
        ln = n // k + (1 if j < n % k else 0)
        out.append((s, ln))
        s += ln
    return out


class B:
    ENGINES = ("tensor", "vector", "scalar", "gpsimd", "sync")

    def __init__(self, nc, stack, simsafe=False):
        self.nc = nc
        self.stack = stack
        self.recs = {e: [] for e in self.ENGINES}
        self.sems = {}
        self.simsafe = simsafe
        self.phase = "init"

    def _sem(self, name):
        if name not in self.sems:
            h = self.stack.enter_context(self.nc.semaphore(name))
            self.sems[name] = [h, 0]
        return self.sems[name]

    def op(self, eng, fn, waits=(), inc=None, inc_by=1):
        ev = None
        incs = []
        if inc is not None:
            s = self._sem(inc)
            s[1] += inc_by if inc_by > 0 else 1
            ev = (s, s[1])
            incs.append((s, inc_by))
        kind = "cc" if inc_by == 0 else ("dma" if inc_by == 16 else "op")
        if inc is None and self.simsafe and fn is not None:
            s = self._sem("ord_" + eng)
            s[1] += 1
            ev = (s, s[1])
            incs.append((s, 1))
        self.recs[eng].append((fn, [w for w in waits if w is not None], incs,
                               kind, ev, self.phase))
        return ev

    def wait_only(self, eng, waits):
        self.recs[eng].append((None, [w for w in waits if w is not None], [],
                               "wait", None, self.phase))

    def dma(self, eng, out, in_, inc, waits=(), **kw):
        return self.op(eng, lambda e: e.dma_start(out=out, in_=in_, **kw),
                       waits=waits, inc=inc, inc_by=16)

    def cc(self, kind, aluop, groups, ins, outs, inc, waits=()):
        return self.op(
            "gpsimd",
            lambda e: e.collective_compute(kind, aluop, replica_groups=groups,
                                           ins=ins, outs=outs),
            waits=waits, inc=inc, inc_by=0)

    def emit(self, block):
        seen = {}

        def runner(eng_name):
            def body(e):
                prev_ev = None
                prev_cev = None
                for (fn, waits, incs, kind, ev, ph) in self.recs[eng_name]:
                    do_ord = (self.simsafe is True or
                              (isinstance(self.simsafe,
                                          (set, frozenset, list, tuple))
                               and (eng_name in self.simsafe or
                                    ph in self.simsafe or
                                    (ph + ":" + eng_name) in self.simsafe)))
                    comp_ord = (kind == "op" and
                                isinstance(self.simsafe,
                                           (set, frozenset, list, tuple)) and
                                (("compute" in self.simsafe and
                                  eng_name in ("vector", "scalar", "gpsimd"))
                                 or "simdbg" in self.simsafe))
                    allw = list(waits)
                    if do_ord and prev_ev is not None and fn is not None:
                        allw.append(prev_ev)
                    elif comp_ord and prev_cev is not None and fn is not None:
                        allw.append(prev_cev)
                    for (s, val) in allw:
                        key = (eng_name, id(s))
                        if seen.get(key, 0) >= val:
                            continue
                        e.wait_ge(s[0], val)
                        seen[key] = val
                    if fn is None:
                        continue
                    ins = fn(e)
                    for (s, by) in incs:
                        if by == 0:
                            ins.then_inc(s[0])
                        else:
                            ins.then_inc(s[0], by)
                    if ev is not None:
                        prev_ev = ev
                        if kind == "op":
                            prev_cev = ev
            return body
        block.tensor(runner("tensor"))
        block.vector(runner("vector"))
        block.scalar(runner("scalar"))
        block.gpsimd(runner("gpsimd"))
        block.sync(runner("sync"))


def build_program(cfg, simsafe=False):
    D, TC, I = cfg["D"], cfg["TC"], cfg["I"]
    NCORE, CPB, TQB, ICH = cfg["NCORE"], cfg["CPB"], cfg["TQB"], cfg["ICH"]
    HD = 128
    NH = D // HD
    NKV = cfg["KVHEADS"]
    GQ = NH // NKV
    KV = NKV * HD
    S = CPB * TC
    HB = D // 128
    HBH = HB // 2
    STT = S // 128
    IT = I // 128
    TBW = min(512, TC)
    NTB = TC // TBW
    NQS = TQB // 128
    NTQB = TC // TQB
    HSL = D // NCORE
    HBL = HSL // 128
    VW = NKV * (HD + 1)
    ich = _chunks(IT, ICH)
    maxit = max(ln for _, ln in ich)
    DQQ = cfg["DQQ"]
    kvgroups = [list(range(g * CPB, (g + 1) * CPB))
                for g in range(NCORE // CPB)]
    allg = [list(range(NCORE))]
    scale_qk = float(1.0 / np.sqrt(HD))
    n_lnb = TC // LNB

    nc = bass.Bass("TRN2", target_bir_lowering=False, debug=False)

    xT = nc.dram_tensor("xT", [D, TC], F32, kind="ExternalInput").ap()
    wqT = nc.dram_tensor("wqT", [D, D], F32, kind="ExternalInput").ap()
    wkT = nc.dram_tensor("wkT", [KV, D], F32, kind="ExternalInput").ap()
    wvT = nc.dram_tensor("wvT", [KV, D], F32, kind="ExternalInput").ap()
    woT = nc.dram_tensor("woT", [D, D], F32, kind="ExternalInput").ap()
    gc_ = nc.dram_tensor("gc", [HSL, I], I32, kind="ExternalInput").ap()
    gs_ = nc.dram_tensor("gs", [HSL, I], F32, kind="ExternalInput").ap()
    uc_ = nc.dram_tensor("uc", [HSL, I], I32, kind="ExternalInput").ap()
    us_ = nc.dram_tensor("us", [HSL, I], F32, kind="ExternalInput").ap()
    dc_ = nc.dram_tensor("dc", [I, HSL], I32, kind="ExternalInput").ap()
    ds_ = nc.dram_tensor("ds", [I, HSL], F32, kind="ExternalInput").ap()
    oT = nc.dram_tensor("oT", [D, TC], F32, kind="ExternalOutput").ap()

    qT_d = nc.dram_tensor("qT_d", [D, TC], BF16).ap()
    hoT_d = nc.dram_tensor("hoT_d", [D, TC], BF16).ap()
    part_d = nc.dram_tensor("part_d", [D, TC], F32).ap()
    kT_in = nc.dram_tensor("kT_in", [KV, TC], BF16).ap()
    kT_out = nc.dram_tensor("kT_out", [CPB * KV, TC], BF16).ap()
    v_in = nc.dram_tensor("v_in", [TC, VW], BF16).ap()
    v_out = nc.dram_tensor("v_out", [CPB * TC, VW], BF16).ap()
    qcols_g = [(it0 * 128, itn * 128) for (it0, itn) in _chunks(IT, cfg["DQQ"])]
    wg_in = [nc.dram_tensor("wg_in%d" % qi, [128, qn * HBL], BF16).ap()
             for qi, (q0, qn) in enumerate(qcols_g)]
    wg_out = [nc.dram_tensor("wg_out%d" % qi, [NCORE * 128, qn * HBL],
                             BF16).ap()
              for qi, (q0, qn) in enumerate(qcols_g)]
    wu_in = [nc.dram_tensor("wu_in%d" % qi, [128, qn * HBL], BF16).ap()
             for qi, (q0, qn) in enumerate(qcols_g)]
    wu_out = [nc.dram_tensor("wu_out%d" % qi, [NCORE * 128, qn * HBL],
                             BF16).ap()
              for qi, (q0, qn) in enumerate(qcols_g)]
    wd_in = [nc.dram_tensor("wd_in%d" % ci, [128, citn * HBL * 128],
                            BF16).ap()
             for ci, (cit0, citn) in enumerate(ich)]
    wd_out = [nc.dram_tensor("wd_out%d" % ci, [NCORE * 128,
                                               citn * HBL * 128], BF16).ap()
              for ci, (cit0, citn) in enumerate(ich)]

    from contextlib import ExitStack
    with ExitStack() as stack:
        e_ = stack.enter_context

        _qc = [(it0 * 128, itn * 128) for (it0, itn) in _chunks(IT, DQQ)]
        _gsz = max(1, (max(qn for _, qn in _qc)) // HSL)
        dqq = max(max(qn for _, qn in _qc), _gsz * HSL)
        NQSL = 8
        ySz = max(3 * dqq, 2 * maxit * 128 + 2 * HB * 128)
        zAttn = (2 * STT * 128 + 2 * STT * (HD + 1) + NQSL * TQB +
                 2 * STT * TQB + 4 * 128 + 6 * TQB)
        zProj = 2 * (HB * 128) + 2 * TBW + 2 * VW
        zFfn = maxit * TC + 8 * HBH * 128 + 2 * TBW
        zSz = max(zAttn, zProj, zFfn)
        nbf = HB * TC + ySz + zSz
        A = e_(nc.sbuf_tensor("A", [128, nbf], BF16))
        X0, Y0, Z0 = 0, HB * TC, HB * TC + ySz

        LNE = HB * LNB + 4 * LNB
        fSz = max(LNE + 2 * dqq, 8 * TBW) + 8
        Af = e_(nc.sbuf_tensor("Af", [128, fSz], F32))
        Ai = e_(nc.sbuf_tensor("Ai", [128, 2 * dqq], I32))

        ident = e_(nc.sbuf_tensor("ident", [128, 128], BF16))
        iota_t = e_(nc.sbuf_tensor("iota_t", [128, 128], F32))
        ones_col = e_(nc.sbuf_tensor("ones_col", [128, 1], F32))
        ones_row = e_(nc.sbuf_tensor("ones_row", [1, 128], F32))
        srow = e_(nc.sbuf_tensor("srow", [1, 6 * LNB], F32))
        rloc = e_(nc.sbuf_tensor("rloc", [128, 4], F32))
        P = [e_(nc.psum_tensor(f"ps{i}", [128, 512], F32)) for i in range(7)]
        Pb = e_(nc.psum_tensor("psb", [128, 1024], BF16))

        blk = e_(nc.Block())
        b = B(nc, stack, simsafe=simsafe)
        psfree = {}

        ev_iota = b.op("gpsimd", lambda e: e.iota(
            iota_t[:], pattern=[[1, 128]], base=0, channel_multiplier=-1,
            allow_small_or_imprecise_dtypes=True), inc="gq")
        ev_ident = b.op("vector", lambda e: e.tensor_scalar(
            ident[:], iota_t[:], 0.0, None, ALU.is_equal),
            waits=[ev_iota], inc="vq")
        b.op("vector", lambda e: e.memset(ones_col[:], 1.0), inc="vq")
        ev_ones = b.op("vector", lambda e: e.memset(ones_row[:], 1.0),
                       inc="vq")

        b.phase = "dq"
        cbf = A[:, Y0:Y0 + dqq]
        lut2 = [A[:, Y0 + dqq:Y0 + 2 * dqq],
                A[:, Y0 + 3 * dqq:Y0 + 4 * dqq]]
        dtmp = A[:, Y0 + 2 * dqq:Y0 + 3 * dqq]
        ci32 = [Ai[:, 0:dqq], Ai[:, dqq:2 * dqq]]
        LNE = HB * LNB + 4 * LNB
        scl2 = [Af[:, LNE:LNE + dqq], Af[:, LNE + dqq:LNE + 2 * dqq]]

        def dequant_lut(lut, n, waits):
            ev = b.op("vector", lambda e, n=n, lut=lut: e.tensor_scalar(
                lut[:, :n], cbf[:, :n], 0.0, NF4[0], ALU.is_equal, ALU.mult),
                waits=waits, inc="vq")
            for v in range(1, 16):
                evt = b.op("vector", lambda e, v=v, n=n: e.tensor_scalar(
                    dtmp[:, :n], cbf[:, :n], float(v), NF4[v],
                    ALU.is_equal, ALU.mult), inc="vq")
                ev = b.op("vector", lambda e, n=n, lut=lut: e.tensor_add(
                    lut[:, :n], lut[:, :n], dtmp[:, :n]),
                    waits=[evt], inc="vq")
            return ev

        qcols = [(it0 * 128, itn * 128) for (it0, itn) in _chunks(IT, DQQ)]
        ev_cc_w = {}
        ev_scl_used = [None, None]
        ev_lut_used = None
        ev_cbf_used = None
        ev_ci_used = [None, None]
        dq_i = 0

        ev_lut_used2 = [None, None]
        dq_pend = []

        def _flush_dq_writes(keep=0):
            nonlocal ev_lut_used
            while len(dq_pend) > keep:
                (wf, lslot, ev_m) = dq_pend.pop(0)
                ev_w = wf([ev_m])
                ev_lut_used2[lslot] = ev_w
                ev_lut_used = ev_w
                wf2 = dq_pend_cb.pop(0)
                wf2(ev_w)

        dq_pend_cb = []

        def dq_tile(load_fn, n, write_fn, done_cb):
            nonlocal dq_i, ev_cbf_used
            islot = dq_i % 2
            lslot = dq_i % 2
            lut = lut2[lslot]
            ev_ld32 = load_fn(ci32[islot], "d_cb%d" % islot,
                              [ev_ci_used[islot]])
            sslot2 = dq_i % 2
            scl = scl2[sslot2]
            ev_s = load_fn(scl, "d_sc%d" % sslot2,
                           [ev_scl_used[sslot2]], scales=True)
            dq_i += 1
            ev_cast = b.op("vector", lambda e, n=n, islot=islot:
                           e.tensor_copy(cbf[:, :n], ci32[islot][:, :n]),
                           waits=[ev_ld32, ev_cbf_used], inc="vq")
            ev_ci_used[islot] = ev_cast
            ev_l = dequant_lut(lut, n, [ev_cast, ev_lut_used2[lslot]])
            ev_cbf_used = ev_l
            ev_m = b.op("vector",
                        lambda e, n=n, scl=scl, lut=lut: e.tensor_mul(
                            lut[:, :n], lut[:, :n], scl[:, :n]),
                        waits=[ev_l, ev_s], inc="vq")
            ev_scl_used[sslot2] = ev_m
            dq_pend.append((write_fn, lslot, ev_m))
            dq_pend_cb.append(done_cb)
            _flush_dq_writes(keep=1)

        srcs = {"wg": (gc_, gs_), "wu": (uc_, us_), "wd": (dc_, ds_)}
        wins = {"wg": wg_in, "wu": wu_in, "wd": wd_in}
        wouts = {"wg": wg_out, "wu": wu_out, "wd": wd_out}
        qi_of_it = {}
        for qi, (q0, qn) in enumerate(qcols):
            for it in range(q0 // 128, (q0 + qn) // 128):
                qi_of_it[it] = qi
        gsz2 = max(1, dqq // HSL)
        wd_tile_its = []
        wd_tiles_per_ci = {}
        for ci, (cit0, citn) in enumerate(ich):
            s0w = cit0
            while s0w < cit0 + citn:
                itn = min(gsz2, cit0 + citn - s0w)
                wd_tile_its.append((ci, s0w, itn))
                wd_tiles_per_ci[ci] = wd_tiles_per_ci.get(ci, 0) + 1
                s0w += itn

        dq_worklist = []
        ag_waits = {}

        def _mk_gu_tile(wname, qi, hbl):
            (q0, qn) = qcols[qi]
            itnl = qn // 128
            codes, scarr = srcs[wname]
            win = wins[wname][qi]
            w4 = win.rearrange("p (it hbl i) -> p it hbl i",
                               it=itnl, hbl=HBL)

            def run():
                lslot = dq_i % 2
                lut = lut2[lslot]

                def loadf(dst, sem, waits, scales=False):
                    src = scarr if scales else codes
                    return b.dma(
                        "sync", dst[:, :qn],
                        src[hbl * 128:(hbl + 1) * 128, q0:q0 + qn],
                        inc=sem, waits=waits)

                def writef(waits):
                    return b.dma(
                        "sync", w4[:, :, hbl, :],
                        lut[:, :qn].rearrange("p (it i) -> p it i",
                                              i=128),
                        inc="d_wr", waits=waits)

                def done(ev_w):
                    ag_waits.setdefault((wname, qi), []).append(ev_w)
                    if len(ag_waits[(wname, qi)]) == HBL:
                        ev_cc_w[(wname, qi)] = b.cc(
                            "AllGather", ALU.bypass, allg,
                            [win], [wouts[wname][qi]],
                            inc="cc_%s%d" % (wname, qi),
                            waits=ag_waits[(wname, qi)])
                dq_tile(loadf, qn, writef, done)
            return run

        def _mk_wd_tile(ti):
            (ci, it0, itn) = wd_tile_its[ti]
            (cit0, citn) = ich[ci]
            n = itn * HSL
            codes, scarr = srcs["wd"]
            win = wins["wd"][ci]
            w4 = win.rearrange("p (it hbl i) -> p it hbl i",
                               it=citn, hbl=HBL)
            c3 = codes.rearrange("(it p) h -> p it h", p=128)
            s3 = scarr.rearrange("(it p) h -> p it h", p=128)

            def run():
                lslot = dq_i % 2
                lut = lut2[lslot]

                def loadf(dst, sem, waits, scales=False):
                    src = s3 if scales else c3
                    return b.dma(
                        "sync",
                        dst[:, :n].rearrange("p (it h) -> p it h",
                                             h=HSL),
                        src[:, it0:it0 + itn, :], inc=sem, waits=waits)

                def writef(waits):
                    lut4 = lut[:, :n].rearrange(
                        "p (it hbl i) -> p it hbl i", it=itn, hbl=HBL)
                    evw = None
                    for hbl in range(HBL):
                        evw = b.dma(
                            "sync",
                            w4[:, it0 - cit0:it0 - cit0 + itn, hbl, :],
                            lut4[:, :, hbl, :],
                            inc="d_wr", waits=(waits if hbl == 0 else ()))
                    return evw

                def done(ev_w):
                    ag_waits.setdefault(("wd", ci), []).append(ev_w)
                    if len(ag_waits[("wd", ci)]) == wd_tiles_per_ci[ci]:
                        ev_cc_w[("wd", ci)] = b.cc(
                            "AllGather", ALU.bypass, allg,
                            [win], [wouts["wd"][ci]],
                            inc="cc_wd%d" % ci,
                            waits=ag_waits[("wd", ci)])
                dq_tile(loadf, n, writef, done)
            return run

        for qi in range(len(qcols)):
            for wname in ("wg", "wu"):
                for hbl in range(HBL):
                    dq_worklist.append(_mk_gu_tile(wname, qi, hbl))
        for ti in range(len(wd_tile_its)):
            dq_worklist.append(_mk_wd_tile(ti))
        dq_next = [0]

        def emit_dq_tile(k=1):
            for _ in range(k):
                if dq_next[0] < len(dq_worklist):
                    dq_worklist[dq_next[0]]()
                    dq_next[0] += 1

        b.phase = "ln"
        xnT = A[:, X0:X0 + HB * TC]
        xnT3 = xnT.rearrange("p (hb t) -> p hb t", hb=HB)
        xblk = Af[:, 0:HB * LNB]
        xblk3 = xblk.rearrange("p (hb t) -> p hb t", hb=HB)
        xsq = [Af[:, HB * LNB + i * LNB: HB * LNB + (i + 1) * LNB]
               for i in range(2)]
        rstd_b = Af[:, HB * LNB + 2 * LNB: HB * LNB + 3 * LNB]
        mb_b = Af[:, HB * LNB + 3 * LNB: HB * LNB + 4 * LNB]
        xT3 = xT.rearrange("(hb p) t -> p hb t", p=128)

        ev_xblk_free = None
        ev_xn = None
        ev_xn_w = []
        for tb in range(n_lnb):
            t0 = tb * LNB
            ev_ld = b.dma("sync", xblk3, xT3[:, :, t0:t0 + LNB],
                          inc="d_x", waits=[ev_xblk_free])
            ev_sx = None
            for hb in range(HB):
                w0 = [ev_ld, ev_ones, psfree.get(4)] if hb == 0 else ()
                ev_sx = b.op("tensor", lambda e, hb=hb: e.matmul(
                    P[4][0:1, :LNB], ones_col[:], xblk3[:, hb, :],
                    start=(hb == 0), stop=(hb == HB - 1)),
                    waits=w0, inc="tq")
            ev_sq_mm = []
            for hb in range(HB):
                w0 = [ev_ld]
                if hb >= 2:
                    w0.append(ev_sq_mm[hb - 2])
                if hb < 2:
                    w0.append(psfree.get(5))
                evq = b.op("scalar", lambda e, hb=hb: e.activation(
                    xsq[hb % 2], xblk3[:, hb, :], AF.Square),
                    waits=w0, inc="sq")
                ev_sq_mm.append(b.op("tensor", lambda e, hb=hb: e.matmul(
                    P[5][0:1, :LNB], ones_col[:], xsq[hb % 2],
                    start=(hb == 0), stop=(hb == HB - 1)),
                    waits=[evq], inc="tq"))
            mean = srow[:, 0 * LNB:1 * LNB]
            ex2 = srow[:, 1 * LNB:2 * LNB]
            m2 = srow[:, 2 * LNB:3 * LNB]
            var = srow[:, 3 * LNB:4 * LNB]
            lnv = srow[:, 4 * LNB:5 * LNB]
            rstd = srow[:, 5 * LNB:6 * LNB]
            e1 = b.op("vector", lambda e: e.tensor_scalar(
                mean, P[4][0:1, :LNB], 1.0 / D, None, ALU.mult),
                waits=[ev_sx], inc="vq")
            e2 = b.op("vector", lambda e: e.tensor_scalar(
                ex2, P[5][0:1, :LNB], 1.0 / D, None, ALU.mult),
                waits=[ev_sq_mm[-1]], inc="vq")
            psfree[4], psfree[5] = e1, e2
            e3 = b.op("vector", lambda e: e.tensor_mul(m2, mean, mean),
                      waits=[e1], inc="vq")
            e4a = b.op("vector", lambda e: e.tensor_sub(var, ex2, m2),
                       waits=[e2, e3], inc="vq")
            e4 = b.op("vector", lambda e: e.tensor_scalar(
                var, var, EPS, None, ALU.add), waits=[e4a], inc="vq")
            e5 = b.op("scalar", lambda e: e.activation(
                lnv, var, AF.Ln), waits=[e4], inc="sq")
            e6 = b.op("scalar", lambda e: e.activation(
                rstd, lnv, AF.Exp, scale=-0.5), waits=[e5], inc="sq")
            e7 = b.op("vector", lambda e: e.scalar_tensor_tensor(
                m2, mean, -1.0, rstd, ALU.mult, ALU.mult),
                waits=[e6], inc="vq")
            e8 = b.op("tensor", lambda e: e.matmul(
                P[6][:, 0:LNB], ones_row[:], rstd, start=True, stop=True),
                waits=[e6, psfree.get(6)], inc="tq")
            e9 = b.op("tensor", lambda e: e.matmul(
                P[6][:, LNB:2 * LNB], ones_row[:], m2, start=True, stop=True),
                waits=[e7], inc="tq")
            eb1 = b.op("vector", lambda e: e.tensor_copy(
                rstd_b, P[6][:, 0:LNB]), waits=[e8, e9], inc="vq")
            eb2 = b.op("vector", lambda e: e.tensor_copy(
                mb_b, P[6][:, LNB:2 * LNB]), waits=[e9], inc="vq")
            psfree[6] = eb2
            for hb in range(HB):
                evm = b.op("vector", lambda e, hb=hb: e.tensor_mul(
                    xsq[hb % 2], xblk3[:, hb, :], rstd_b),
                    waits=[eb1, eb2], inc="vq")
                ev_xn = b.op("vector", lambda e, hb=hb, t0=t0: e.tensor_add(
                    xnT3[:, hb, t0:t0 + LNB], xsq[hb % 2], mb_b),
                    waits=[evm], inc="vq")
            ev_xblk_free = ev_xn
            ev_xn_w.append(ev_xn)
            if tb % 2 == 1:
                emit_dq_tile(1)

        b.phase = "proj"
        wpan = [A[:, Z0 + i * HB * 128: Z0 + (i + 1) * HB * 128]
                for i in range(2)]
        ps0 = Z0 + 2 * HB * 128
        pstg = [A[:, ps0 + i * TBW: ps0 + (i + 1) * TBW] for i in range(2)]
        vs0 = ps0 + 2 * TBW
        vstg = [A[:, vs0 + i * VW: vs0 + (i + 1) * VW] for i in range(2)]

        ev_vst_used = []
        for i in range(2):
            ev_vst_used.append(b.op(
                "vector", lambda e, i=i: e.memset(
                    vstg[i].rearrange("p (h c) -> p h c", c=HD + 1)[:, :, HD:],
                    1.0), inc="vq"))

        def cast_panel(slot, wT, dt, waits, sem):
            return b.dma("gpsimd", wpan[slot][:, :HB * 128],
                         wT[dt * 128:(dt + 1) * 128, :],
                         inc=sem, waits=waits)

        ev_pan_used = [None, None]
        ev_pstg_used = [None, None]
        kv_evs = []
        q_evs = []
        pidx = 0

        WLN = TBW // LNB
        for dt in range(KV // 128):
            slot = pidx % 2
            ev_pan = cast_panel(slot, wkT, dt, [ev_pan_used[slot]],
                                "d_wp%d" % slot)
            ev_mm = None
            for tb in range(NTB):
                bank = (pidx * NTB + tb) % 4
                for hb in range(HB):
                    w0 = ([ev_pan, ev_xn_w[(tb + 1) * WLN - 1],
                           psfree.get(bank)]
                          if hb == 0 else ())
                    ev_mm = b.op("tensor", lambda e, hb=hb, tb=tb, bank=bank, slot=slot: e.matmul(
                        P[bank][:, :TBW],
                        wpan[slot].rearrange("p (hb d) -> p hb d", hb=HB)[:, hb, :128],
                        xnT3[:, hb, tb * TBW:(tb + 1) * TBW],
                        start=(hb == 0), stop=(hb == HB - 1)),
                        waits=w0, inc="tq")
                sslot = bank % 2
                ev_ev = b.op("scalar", lambda e, bank=bank, sslot=sslot: e.activation(
                    pstg[sslot][:, :TBW], P[bank][:, :TBW], AF.Copy),
                    waits=[ev_mm, ev_pstg_used[sslot]], inc="sq")
                psfree[bank] = ev_ev
                ev_st = b.dma("scalar",
                              kT_in[dt * 128:(dt + 1) * 128,
                                    tb * TBW:(tb + 1) * TBW],
                              pstg[sslot][:, :TBW], inc="d_ps%d" % sslot,
                              waits=[ev_ev])
                ev_pstg_used[sslot] = ev_st
                kv_evs.append(ev_st)
            ev_pan_used[slot] = ev_mm
            pidx += 1
        nwv = 128
        for tt in range(TC // 128):
            for dh in range(KV // nwv):
                slot = pidx % 2
                ev_pan = cast_panel(slot, wvT, dh, [ev_pan_used[slot]],
                                    "d_wp%d" % slot)
                bank = pidx % 4
                ev_mm = None
                for hb in range(HB):
                    w0 = ([ev_pan, ev_xn_w[(tt + 1) * (128 // LNB) - 1],
                           psfree.get(bank)]
                          if hb == 0 else ())
                    ev_mm = b.op("tensor", lambda e, hb=hb, tt=tt, bank=bank, slot=slot, nwv=nwv: e.matmul(
                        P[bank][:, :nwv],
                        xnT3[:, hb, tt * 128:(tt + 1) * 128],
                        wpan[slot].rearrange("p (hb d) -> p hb d", hb=HB)[:, hb, :nwv],
                        start=(hb == 0), stop=(hb == HB - 1)),
                        waits=w0, inc="tq")
                ev_pan_used[slot] = ev_mm
                vslot = pidx % 2
                nevac = nwv // HD
                ev_ev = None
                for j in range(nevac):
                    w0 = [ev_mm, ev_vst_used[vslot]] if j == 0 else ()
                    ev_ev = b.op("scalar", lambda e, j=j, bank=bank, vslot=vslot: e.activation(
                        vstg[vslot].rearrange("p (h c) -> p h c",
                                              c=HD + 1)[:, j, :HD],
                        P[bank][:, j * HD:(j + 1) * HD], AF.Copy),
                        waits=w0, inc="sq")
                psfree[bank] = ev_ev
                ev_st = b.dma(
                    "scalar",
                    v_in[tt * 128:(tt + 1) * 128,
                         dh * nevac * (HD + 1):(dh + 1) * nevac * (HD + 1)],
                    vstg[vslot][:, :nevac * (HD + 1)],
                    inc="d_vs%d" % vslot, waits=[ev_ev])
                ev_vst_used[vslot] = ev_st
                kv_evs.append(ev_st)
                pidx += 1
        ev_cc_k = b.cc("AllGather", ALU.bypass, kvgroups, [kT_in], [kT_out],
                       inc="cck", waits=kv_evs)
        ev_cc_v = b.cc("AllGather", ALU.bypass, kvgroups, [v_in], [v_out],
                       inc="ccv", waits=kv_evs)
        for dt in range(D // 128):
            slot = pidx % 2
            ev_pan = cast_panel(slot, wqT, dt, [ev_pan_used[slot]],
                                "d_wp%d" % slot)
            ev_mm = None
            for tb in range(NTB):
                bank = (pidx * NTB + tb) % 4
                for hb in range(HB):
                    w0 = ([ev_pan, ev_xn_w[(tb + 1) * WLN - 1],
                           psfree.get(bank)]
                          if hb == 0 else ())
                    ev_mm = b.op("tensor", lambda e, hb=hb, tb=tb, bank=bank, slot=slot: e.matmul(
                        P[bank][:, :TBW],
                        wpan[slot].rearrange("p (hb d) -> p hb d", hb=HB)[:, hb, :128],
                        xnT3[:, hb, tb * TBW:(tb + 1) * TBW],
                        start=(hb == 0), stop=(hb == HB - 1)),
                        waits=w0, inc="tq")
                sslot = bank % 2
                ev_ev = b.op("scalar", lambda e, bank=bank, sslot=sslot: e.activation(
                    pstg[sslot][:, :TBW], P[bank][:, :TBW], AF.Copy),
                    waits=[ev_mm, ev_pstg_used[sslot]], inc="sq")
                psfree[bank] = ev_ev
                ev_st = b.dma("scalar",
                              qT_d[dt * 128:(dt + 1) * 128,
                                   tb * TBW:(tb + 1) * TBW],
                              pstg[sslot][:, :TBW], inc="d_ps%d" % sslot,
                              waits=[ev_ev])
                ev_pstg_used[sslot] = ev_st
                q_evs.append(ev_st)
            ev_pan_used[slot] = ev_mm
            pidx += 1

        emit_dq_tile(len(dq_worklist))
        _flush_dq_writes(0)
        ev_dq_done = ev_lut_used

        b.phase = "attn"
        za = Z0
        kTs = [A[:, za + i * STT * 128: za + (i + 1) * STT * 128]
               for i in range(2)]
        za += 2 * STT * 128
        Vs = [A[:, za + i * STT * (HD + 1): za + (i + 1) * STT * (HD + 1)]
              for i in range(2)]
        za += 2 * STT * (HD + 1)
        qTs = [A[:, za + i * TQB: za + (i + 1) * TQB] for i in range(NQSL)]
        za += NQSL * TQB
        expP = [A[:, za + i * STT * TQB: za + (i + 1) * STT * TQB]
                for i in range(2)]
        za += 2 * STT * TQB
        hosc = [A[:, za + i * 128: za + (i + 1) * 128] for i in range(4)]
        za += 4 * 128
        hostg = [A[:, za + i * TQB: za + (i + 1) * TQB] for i in range(6)]

        kT_o3 = kT_out.rearrange("(r d) t -> d r t", r=CPB)
        v_o3 = v_out.rearrange("(tt p) c -> p tt c", p=128)

        ev_att_by_u = []
        ev_sc_last = {}
        ev_q_used = [None] * NQSL
        ev_hosc_used = [None, None, None, None]
        ev_hostg_used = [None] * 6
        ev_trev_prev = [None, None]
        ho_evs = []
        pend = None
        u = 0

        def emit_epilogue(st, ev_avs):
            nonlocal ev_trev_prev
            (qh, tqb, es, ks, ev_exp_u, ev_lv, uu) = st
            avbs = [4, 5] if uu % 2 == 0 else [3, 6]
            ev_tr = None
            for tqs in range(NQS):
                avb = avbs[tqs % 2]
                el = b.op("scalar", lambda e, tqs=tqs, avb=avb: e.activation(
                    rloc[:, tqs:tqs + 1], P[avb][:, HD:HD + 1],
                    AF.Ln), waits=[ev_avs[tqs]], inc="sq")
                er = b.op("scalar", lambda e, tqs=tqs: e.activation(
                    rloc[:, 2 + tqs:3 + tqs], rloc[:, tqs:tqs + 1],
                    AF.Exp, scale=-1.0), waits=[el], inc="sq")
                hs = (uu % 2) * 2 + (tqs % 2)
                eh = b.op("scalar", lambda e, tqs=tqs, avb=avb, hs=hs: e.activation(
                    hosc[hs], P[avb][:, :HD], AF.Copy,
                    scale=rloc[:, 2 + tqs:3 + tqs]),
                    waits=[er, ev_hosc_used[hs]], inc="sq")
                psfree[avb] = eh
                pbh = (uu % 2) * NQS * 128
                ev_tr = b.op("tensor", lambda e, tqs=tqs, hs=hs, pbh=pbh: e.transpose(
                    Pb[:, pbh + tqs * 128:pbh + (tqs + 1) * 128], hosc[hs],
                    ident[:]),
                    waits=[eh, ev_ident, ev_trev_prev[uu % 2]], inc="tq")
                ev_hosc_used[hs] = ev_tr
            ev_att_by_u.append(ev_avs[-1])
            hss = len(ev_att_by_u) % 6
            pbh = (uu % 2) * NQS * 128
            ev_he = b.op("scalar", lambda e, hss=hss, pbh=pbh: e.activation(
                hostg[hss][:, :TQB], Pb[:, pbh:pbh + TQB], AF.Copy),
                waits=[ev_tr, ev_hostg_used[hss]], inc="sq")
            ev_trev_prev[uu % 2] = ev_he
            ev_hst = b.dma(
                "scalar",
                hoT_d[qh * 128:(qh + 1) * 128,
                      tqb * TQB:(tqb + 1) * TQB],
                hostg[hss][:, :TQB], inc="d_h%d" % hss,
                waits=[ev_he])
            ev_hostg_used[hss] = ev_hst
            ho_evs.append(ev_hst)

        def emit_av_pair(st, pair, ev_avs):
            (qh, tqb, es, ks, ev_exp_u, ev_lv, uu) = st
            avbs = [4, 5] if uu % 2 == 0 else [3, 6]
            for tqs in range(NQS):
                avb = avbs[tqs % 2]
                for tkt in (2 * pair, 2 * pair + 1):
                    w0 = []
                    if tqs == 0:
                        w0.append(ev_exp_u[pair])
                    if tkt == 0:
                        w0 += [ev_lv, psfree.get(avb)]
                    ev_avs[tqs] = b.op("tensor", lambda e, tkt=tkt, tqs=tqs, avb=avb, es=es, ks=ks: e.matmul(
                        P[avb][:, :HD + 1],
                        expP[es][:, tkt * TQB + tqs * 128:
                                 tkt * TQB + tqs * 128 + 128],
                        Vs[ks].rearrange("p (tt c) -> p tt c",
                                         c=HD + 1)[:, tkt, :],
                        start=(tkt == 0), stop=(tkt == STT - 1),
                        skip_group_check=True),
                        waits=w0, inc="tq")

        units = [(kvh, g, tqb, kvh * GQ + g)
                 for kvh in range(NKV) for g in range(GQ)
                 for tqb in range(NTQB)]

        ev_lk = {}
        ev_lv = {}
        ev_lq_by_u = {}

        def load_kv(kvh):
            ks = kvh % 2
            ev_lk[kvh] = b.dma(
                "scalar", kTs[ks].rearrange("p (r t) -> p r t", r=CPB),
                kT_o3[kvh * 128:(kvh + 1) * 128, :, :],
                inc="d_k%d" % ks,
                waits=[ev_cc_k, ev_sc_last.get(kvh - 2)] + q_evs)
            wv_v = [ev_cc_v] + q_evs
            if kvh >= 2:
                wv_v.append(ev_att_by_u[(kvh - 1) * GQ * NTQB - 1])
            ev_lv[kvh] = b.dma(
                "scalar", Vs[ks].rearrange("p (tt c) -> p tt c", c=HD + 1),
                v_o3[:, :, kvh * (HD + 1):(kvh + 1) * (HD + 1)],
                inc="d_v%d" % ks, waits=wv_v)

        def load_q(uu):
            qh = units[uu][3]
            tqb = units[uu][2]
            qs = uu % NQSL
            ev_lq_by_u[uu] = b.dma(
                "scalar", qTs[qs],
                qT_d[qh * 128:(qh + 1) * 128, tqb * TQB:(tqb + 1) * TQB],
                inc="d_q%d" % qs,
                waits=[ev_q_used[qs]] + q_evs)

        load_kv(0)
        for w in range(min(NQSL, len(units))):
            load_q(w)
        for u, (kvh, g, tqb, qh) in enumerate(units):
            ks = kvh % 2
            qs = u % NQSL
            es = u % 2
            if g == 0 and tqb == min(1, NTQB - 1) and kvh + 1 < NKV:
                load_kv(kvh + 1)
            ev_exp_u = []
            ev_avs = [None] * NQS
            ev_mm = None
            for tkt in range(STT):
                pair, half = tkt // 2, tkt % 2
                bank = pair % 3
                w0 = [ev_lk[kvh], ev_lq_by_u[u]]
                if half == 0:
                    w0.append(psfree.get(bank))
                ev_mm = b.op("tensor", lambda e, tkt=tkt, bank=bank, half=half, ks=ks, qs=qs: e.matmul(
                    P[bank][:, half * TQB:(half + 1) * TQB],
                    kTs[ks][:, tkt * 128:(tkt + 1) * 128],
                    qTs[qs][:, :TQB], start=True, stop=True),
                    waits=w0, inc="tq")
                ev_sc_last[kvh] = ev_mm
                if half == 1:
                    eex = b.op("scalar", lambda e, pair=pair, bank=bank, es=es: e.activation(
                        expP[es][:, 2 * pair * TQB:2 * (pair + 1) * TQB],
                        P[bank][:, :2 * TQB], AF.Exp, scale=scale_qk),
                        waits=[ev_mm], inc="sq")
                    ev_exp_u.append(eex)
                    psfree[bank] = eex
                    if pend is not None:
                        emit_av_pair(pend[0], pair, pend[1])
            ev_q_used[qs] = ev_mm
            if u + NQSL < len(units):
                load_q(u + NQSL)
            if pend is not None:
                emit_epilogue(pend[0], pend[1])
            pend = ((qh, tqb, es, ks, ev_exp_u, ev_lv[kvh], u), ev_avs)
        for pair in range(STT // 2):
            emit_av_pair(pend[0], pair, pend[1])
        emit_epilogue(pend[0], pend[1])

        b.phase = "ffn"
        zf = Z0
        intT = A[:, zf:zf + maxit * TC]
        zf += maxit * TC
        gpan = [A[:, zf + i * HBH * 128: zf + (i + 1) * HBH * 128]
                for i in range(8)]
        zf += 8 * HBH * 128
        sg = [A[:, zf + i * TBW: zf + (i + 1) * TBW] for i in range(2)]
        wdp = [A[:, Y0 + i * maxit * 128: Y0 + (i + 1) * maxit * 128]
               for i in range(2)]
        wop = [A[:, Y0 + 2 * maxit * 128 + i * HB * 128:
                 Y0 + 2 * maxit * 128 + (i + 1) * HB * 128]
               for i in range(2)]
        hoT_sb = A[:, X0:X0 + HB * TC]
        hoT_sb3 = hoT_sb.rearrange("p (hb t) -> p hb t", hb=HB)
        pastg = [Af[:, i * TBW:(i + 1) * TBW] for i in range(2)]
        outstg = [Af[:, (2 + i) * TBW:(3 + i) * TBW] for i in range(2)]
        xr = [Af[:, (4 + i) * TBW:(5 + i) * TBW] for i in range(4)]
        upstg = [Af[:, (8 + i) * TBW:(9 + i) * TBW] for i in range(2)]

        wg_o = [w.rearrange("(r p) (it q) -> p it r q", p=128,
                            it=qn // 128)
                for w, (q0, qn) in zip(wg_out, qcols)]
        wu_o = [w.rearrange("(r p) (it q) -> p it r q", p=128,
                            it=qn // 128)
                for w, (q0, qn) in zip(wu_out, qcols)]
        wd_o = [w.rearrange("(r p) (it hbl i) -> p r it hbl i",
                            p=128, it=citn, hbl=HBL)
                for w, (cit0, citn) in zip(wd_out, ich)]



        ev_gpan_used = [None] * 8
        ev_sg_used = [None, None]
        ev_ups_used = [ev_scl_used[0], ev_scl_used[1]]
        ev_int_rd = None
        ev_wdp_used = [None, None]
        ev_wop_used = [None, None]
        ev_stg_used = [None, None]
        ev_xr_used = [None] * 4
        ev_xr_ld = {}
        ev_pa_st = {}
        out_evs = []
        ffn_first = [ev_xn]
        dn_pair = [(4, 5), (0, 1)]

        def load_gu(kind, it, itl):
            evs = []
            qi = qi_of_it[it]
            itloc = it - qcols[qi][0] // 128
            src = (wg_o if kind == 0 else wu_o)[qi]
            ccw = ev_cc_w[("wg" if kind == 0 else "wu", qi)]
            for half in range(2):
                slot = (itl % 2) * 4 + kind * 2 + half
                rh = NCORE // 2
                ev = b.dma(
                    "scalar",
                    gpan[slot].rearrange("p (r q) -> p r q", r=rh),
                    src[:, itloc, half * rh:(half + 1) * rh, :],
                    inc="d_gp%d" % slot,
                    waits=[ccw, ev_gpan_used[slot]] + ho_evs)
                evs.append(ev)
            return evs

        def load_dn(ci, ht, cit0, citn, last_chunk):
            wslot = ht % 2
            ev_wd = b.dma("sync",
                          wdp[wslot][:, :citn * 128].rearrange(
                              "p (it i) -> p it i", i=128),
                          wd_o[ci][:, ht // HBL, :, ht % HBL, :],
                          inc="d_wdl%d" % wslot,
                          waits=[ev_cc_w[("wd", ci)], ev_dq_done,
                                 ev_wdp_used[wslot]])
            ev_wo = None
            if last_chunk:
                ev_wo = b.dma("gpsimd",
                              wop[wslot][:, :HB * 128],
                              woT[ht * 128:(ht + 1) * 128, :],
                              inc="d_wo%d" % wslot,
                              waits=[ev_wop_used[wslot], ev_dq_done])
            for tb in range(NTB):
                xslot = (ht % 2) * 2 + tb
                if ci == 0:
                    src = xT[ht * 128:(ht + 1) * 128,
                             tb * TBW:(tb + 1) * TBW]
                    w0 = [ev_xr_used[xslot], ev_dq_done]
                else:
                    src = part_d[ht * 128:(ht + 1) * 128,
                                 tb * TBW:(tb + 1) * TBW]
                    w0 = [ev_xr_used[xslot]] + ev_pa_st[(ci - 1, ht)]
                ev_xr_ld[(ht, tb)] = b.dma("sync", xr[xslot], src,
                                           inc="d_xr%d" % xslot, waits=w0)
            return ev_wd, ev_wo

        for ci, (cit0, citn) in enumerate(ich):
            last_chunk = (ci == ICH - 1)
            ev_dve_int = None
            gate_mm = [None] * NTB
            up_mm = [None] * NTB
            for itl in range(citn):
                it = cit0 + itl
                pg = load_gu(0, it, itl)
                pu = load_gu(1, it, itl)
                for kind in range(2):
                    pan = pg if kind == 0 else pu
                    for tb in range(NTB):
                        bank = kind * 2 + (tb % 2)
                        ev_mm = None
                        for hb in range(HB):
                            half, hloc = divmod(hb, HBH)
                            slot = (itl % 2) * 4 + kind * 2 + half
                            w0 = ()
                            if hb == 0:
                                w0 = [pan[0], psfree.get(bank)] + ffn_first
                                if itl == 0 and tb == 0 and kind == 0:
                                    w0 = w0 + [ev_int_rd]
                            elif hb == HBH:
                                w0 = [pan[1]]
                            ev_mm = b.op(
                                "tensor",
                                lambda e, hb=hb, hloc=hloc, slot=slot,
                                bank=bank, tb=tb: e.matmul(
                                    P[bank][:, :TBW],
                                    gpan[slot].rearrange(
                                        "p (hb i) -> p hb i",
                                        hb=HBH)[:, hloc, :],
                                    xnT3[:, hb, tb * TBW:(tb + 1) * TBW],
                                    start=(hb == 0), stop=(hb == HB - 1)),
                                waits=w0, inc="tq")
                        ffn_first = []
                        if kind == 0:
                            gate_mm[tb] = ev_mm
                        else:
                            up_mm[tb] = ev_mm
                for j in range(4):
                    ev_gpan_used[(itl % 2) * 4 + j] = up_mm[NTB - 1]
                for tb in range(NTB):
                    sslot = (itl * NTB + tb) % 2
                    ev_si = b.op("scalar", lambda e, tb=tb, sslot=sslot:
                                 e.activation(sg[sslot][:, :TBW],
                                              P[tb % 2][:, :TBW], AF.Silu),
                                 waits=[gate_mm[tb], ev_sg_used[sslot]],
                                 inc="sq")
                    psfree[tb % 2] = ev_si
                    ev_up = b.op("scalar", lambda e, tb=tb, sslot=sslot:
                                 e.activation(upstg[sslot][:, :TBW],
                                              P[2 + (tb % 2)][:, :TBW],
                                              AF.Copy),
                                 waits=[up_mm[tb], ev_ups_used[sslot]],
                                 inc="sq")
                    psfree[2 + (tb % 2)] = ev_up
                    ev_dve_int = b.op(
                        "gpsimd", lambda e, itl=itl, tb=tb, sslot=sslot:
                        e.tensor_mul(
                            intT[:, itl * TC + tb * TBW:
                                 itl * TC + (tb + 1) * TBW],
                            sg[sslot][:, :TBW], upstg[sslot][:, :TBW]),
                        waits=[ev_si, ev_up], inc="gq")
                    ev_sg_used[sslot] = ev_dve_int
                    ev_ups_used[sslot] = ev_dve_int
            ev_gu_done = up_mm[NTB - 1]
            ev_hoT_ld = None
            if last_chunk:
                ev_hoT_ld = b.dma("sync", hoT_sb3,
                                  hoT_d.rearrange("(hb p) t -> p hb t",
                                                  p=128),
                                  inc="d_hol",
                                  waits=[ev_gu_done] + ho_evs)
            ev_wd, ev_wo = load_dn(ci, 0, cit0, citn, last_chunk)
            for ht in range(HB):
                wslot = ht % 2
                ev_wd_cur, ev_wo_cur = ev_wd, ev_wo
                if ht + 1 < HB:
                    ev_wd, ev_wo = load_dn(ci, ht + 1, cit0, citn,
                                           last_chunk)
                for tb in range(NTB):
                    bank = dn_pair[ht % 2][tb]
                    ev_mm = None
                    for itl in range(citn):
                        w0 = ()
                        if itl == 0:
                            w0 = [ev_wd_cur, psfree.get(bank), ev_dve_int]
                        ev_mm = b.op(
                            "tensor",
                            lambda e, itl=itl, tb=tb, bank=bank, wslot=wslot,
                            citn=citn, last_chunk=last_chunk: e.matmul(
                                P[bank][:, :TBW],
                                wdp[wslot][:, itl * 128:(itl + 1) * 128],
                                intT[:, itl * TC + tb * TBW:
                                     itl * TC + (tb + 1) * TBW],
                                start=(itl == 0),
                                stop=(itl == citn - 1 and not last_chunk)),
                            waits=w0, inc="tq")
                    if last_chunk:
                        for hb in range(HB):
                            w0 = [ev_wo_cur, ev_hoT_ld] if hb == 0 else ()
                            ev_mm = b.op(
                                "tensor",
                                lambda e, hb=hb, tb=tb, bank=bank,
                                wslot=wslot: e.matmul(
                                    P[bank][:, :TBW],
                                    wop[wslot].rearrange(
                                        "p (hb i) -> p hb i",
                                        hb=HB)[:, hb, :],
                                    hoT_sb3[:, hb,
                                            tb * TBW:(tb + 1) * TBW],
                                    start=False, stop=(hb == HB - 1)),
                                waits=w0, inc="tq")
                    ev_int_rd = ev_mm
                    ev_wdp_used[wslot] = ev_mm
                    if last_chunk:
                        ev_wop_used[wslot] = ev_mm
                    stg = tb % 2
                    xslot = (ht % 2) * 2 + tb
                    dststg = outstg[stg] if last_chunk else pastg[stg]
                    ev_e = b.op(
                        "vector",
                        lambda e, bank=bank, dststg=dststg, xslot=xslot:
                        e.tensor_add(dststg, P[bank][:, :TBW], xr[xslot]),
                        waits=[ev_mm, ev_xr_ld[(ht, tb)],
                               ev_stg_used[stg]], inc="vq")
                    psfree[bank] = ev_e
                    ev_xr_used[xslot] = ev_e
                    dst = (oT if last_chunk else part_d)[
                        ht * 128:(ht + 1) * 128, tb * TBW:(tb + 1) * TBW]
                    ev_s = b.dma("sync", dst, dststg,
                                 inc=("d_out%d" % stg if last_chunk
                                      else "d_pa%d" % stg),
                                 waits=[ev_e])
                    ev_stg_used[stg] = ev_s
                    if last_chunk:
                        out_evs.append(ev_s)
                    else:
                        ev_pa_st.setdefault((ci, ht), []).append(ev_s)

        b.wait_only("sync", out_evs)
        b.emit(blk)
    return nc


def _prep_inputs(x, Wq, Wk, Wv, Wo, gcodes, gscales, ucodes, uscales,
                 dcodes, dscales, cfg):
    D, TC, NCORE = cfg["D"], cfg["TC"], cfg["NCORE"]
    HSL = D // NCORE
    BLK = 64
    S_all = x.reshape(-1, D)
    xTf = np.ascontiguousarray(S_all.T.astype(np.float32))

    def _pl(wT):
        Kd, Md = wT.shape
        return np.ascontiguousarray(
            wT.reshape(Kd // 128, 128, Md // 128, 128)
            .transpose(2, 1, 0, 3).reshape(Md, Kd))
    wqT = _pl(Wq.T)
    wkT = _pl(Wk.T)
    wvT = _pl(Wv.T)
    woT = _pl(Wo.T)
    gcT = np.ascontiguousarray(gcodes.T)
    ucT = np.ascontiguousarray(ucodes.T)
    gsx = np.repeat(np.ascontiguousarray(gscales.T), BLK, axis=0)
    usx = np.repeat(np.ascontiguousarray(uscales.T), BLK, axis=0)
    dcT = np.ascontiguousarray(dcodes.T)
    maps = []
    for c in range(NCORE):
        dsx_c = np.repeat(
            np.ascontiguousarray(dscales.T[:, c * HSL:(c + 1) * HSL]),
            BLK, axis=0)
        maps.append({
            "xT": np.ascontiguousarray(xTf[:, c * TC:(c + 1) * TC]),
            "wqT": wqT, "wkT": wkT, "wvT": wvT, "woT": woT,
            "gc": np.ascontiguousarray(gcT[c * HSL:(c + 1) * HSL]),
            "gs": np.ascontiguousarray(gsx[c * HSL:(c + 1) * HSL]),
            "uc": np.ascontiguousarray(ucT[c * HSL:(c + 1) * HSL]),
            "us": np.ascontiguousarray(usx[c * HSL:(c + 1) * HSL]),
            "dc": np.ascontiguousarray(dcT[:, c * HSL:(c + 1) * HSL]),
            "ds": np.ascontiguousarray(dsx_c),
        })
    return maps


def run_cfg(inputs_dict, cfg, simsafe=False):
    maps = _prep_inputs(cfg=cfg, **inputs_dict)
    nc = build_program(cfg, simsafe=simsafe)
    res = run_bass_kernel_spmd(nc, maps, core_ids=list(range(cfg["NCORE"])))
    outs = [np.asarray(r["oT"], np.float32) for r in res.results]
    full = np.concatenate(outs, axis=1)
    return np.ascontiguousarray(full.T)


def kernel(L0_attn_x, L0_attn_Wq, L0_attn_Wk, L0_attn_Wv, L0_attn_Wo,
           L0_ffn_gate_codes, L0_ffn_gate_scales, L0_ffn_up_codes,
           L0_ffn_up_scales, L0_ffn_down_codes, L0_ffn_down_scales):
    cfg = FULL_CFG
    x = np.asarray(L0_attn_x, np.float32)
    B_, T_, D_ = x.shape
    out = run_cfg(cfg=cfg, simsafe=False, inputs_dict=dict(
        x=x,
        Wq=np.asarray(L0_attn_Wq, np.float32),
        Wk=np.asarray(L0_attn_Wk, np.float32),
        Wv=np.asarray(L0_attn_Wv, np.float32),
        Wo=np.asarray(L0_attn_Wo, np.float32),
        gcodes=np.asarray(L0_ffn_gate_codes, np.int32),
        gscales=np.asarray(L0_ffn_gate_scales, np.float32),
        ucodes=np.asarray(L0_ffn_up_codes, np.int32),
        uscales=np.asarray(L0_ffn_up_scales, np.float32),
        dcodes=np.asarray(L0_ffn_down_codes, np.int32),
        dscales=np.asarray(L0_ffn_down_scales, np.float32),
    ))
    return out.reshape(B_, T_, D_)


# version 20
# speedup vs baseline: 1.0580x; 1.0242x over previous
import numpy as np
import ml_dtypes
import concourse.bass as bass
import concourse.mybir as mybir
from concourse.bass_utils import run_bass_kernel_spmd

F32 = mybir.dt.float32
BF16 = mybir.dt.bfloat16
I32 = mybir.dt.int32
AF = mybir.ActivationFunctionType
ALU = mybir.AluOpType

NF4 = [-1.0, -0.6961928009986877, -0.5250730514526367, -0.39491748809814453,
       -0.28444138169288635, -0.18477343022823334, -0.09105003625154495, 0.0,
       0.07958029955625534, 0.16093020141124725, 0.24611230194568634,
       0.33791524171829224, 0.44070982933044434, 0.5626170039176941,
       0.7229568362236023, 1.0]

FULL_CFG = dict(D=4096, TC=1024, I=11008, NCORE=8, CPB=4, TQB=256, ICH=5,
                KVHEADS=8, DQQ=6)
EPS = 1e-5
LNB = 64


def _chunks(n, k):
    out, s = [], 0
    for j in range(k):
        ln = n // k + (1 if j < n % k else 0)
        out.append((s, ln))
        s += ln
    return out


class B:
    ENGINES = ("tensor", "vector", "scalar", "gpsimd", "sync")

    def __init__(self, nc, stack, simsafe=False):
        self.nc = nc
        self.stack = stack
        self.recs = {e: [] for e in self.ENGINES}
        self.sems = {}
        self.simsafe = simsafe
        self.phase = "init"

    def _sem(self, name):
        if name not in self.sems:
            h = self.stack.enter_context(self.nc.semaphore(name))
            self.sems[name] = [h, 0]
        return self.sems[name]

    def op(self, eng, fn, waits=(), inc=None, inc_by=1):
        ev = None
        incs = []
        if inc is not None:
            s = self._sem(inc)
            s[1] += inc_by if inc_by > 0 else 1
            ev = (s, s[1])
            incs.append((s, inc_by))
        kind = "cc" if inc_by == 0 else ("dma" if inc_by == 16 else "op")
        if inc is None and self.simsafe and fn is not None:
            s = self._sem("ord_" + eng)
            s[1] += 1
            ev = (s, s[1])
            incs.append((s, 1))
        self.recs[eng].append((fn, [w for w in waits if w is not None], incs,
                               kind, ev, self.phase))
        return ev

    def wait_only(self, eng, waits):
        self.recs[eng].append((None, [w for w in waits if w is not None], [],
                               "wait", None, self.phase))

    def dma(self, eng, out, in_, inc, waits=(), **kw):
        return self.op(eng, lambda e: e.dma_start(out=out, in_=in_, **kw),
                       waits=waits, inc=inc, inc_by=16)

    def cc(self, kind, aluop, groups, ins, outs, inc, waits=()):
        return self.op(
            "gpsimd",
            lambda e: e.collective_compute(kind, aluop, replica_groups=groups,
                                           ins=ins, outs=outs),
            waits=waits, inc=inc, inc_by=0)

    def emit(self, block):
        seen = {}

        def runner(eng_name):
            def body(e):
                prev_ev = None
                prev_cev = None
                for (fn, waits, incs, kind, ev, ph) in self.recs[eng_name]:
                    do_ord = (self.simsafe is True or
                              (isinstance(self.simsafe,
                                          (set, frozenset, list, tuple))
                               and (eng_name in self.simsafe or
                                    ph in self.simsafe or
                                    (ph + ":" + eng_name) in self.simsafe)))
                    comp_ord = (kind == "op" and
                                isinstance(self.simsafe,
                                           (set, frozenset, list, tuple)) and
                                (("compute" in self.simsafe and
                                  eng_name in ("vector", "scalar", "gpsimd"))
                                 or "simdbg" in self.simsafe))
                    allw = list(waits)
                    if do_ord and prev_ev is not None and fn is not None:
                        allw.append(prev_ev)
                    elif comp_ord and prev_cev is not None and fn is not None:
                        allw.append(prev_cev)
                    for (s, val) in allw:
                        key = (eng_name, id(s))
                        if seen.get(key, 0) >= val:
                            continue
                        e.wait_ge(s[0], val)
                        seen[key] = val
                    if fn is None:
                        continue
                    ins = fn(e)
                    for (s, by) in incs:
                        if by == 0:
                            ins.then_inc(s[0])
                        else:
                            ins.then_inc(s[0], by)
                    if ev is not None:
                        prev_ev = ev
                        if kind == "op":
                            prev_cev = ev
            return body
        block.tensor(runner("tensor"))
        block.vector(runner("vector"))
        block.scalar(runner("scalar"))
        block.gpsimd(runner("gpsimd"))
        block.sync(runner("sync"))


def build_program(cfg, simsafe=False):
    D, TC, I = cfg["D"], cfg["TC"], cfg["I"]
    NCORE, CPB, TQB, ICH = cfg["NCORE"], cfg["CPB"], cfg["TQB"], cfg["ICH"]
    HD = 128
    NH = D // HD
    NKV = cfg["KVHEADS"]
    GQ = NH // NKV
    KV = NKV * HD
    S = CPB * TC
    HB = D // 128
    HBH = HB // 2
    STT = S // 128
    IT = I // 128
    TBW = min(512, TC)
    NTB = TC // TBW
    NQS = TQB // 128
    NTQB = TC // TQB
    HSL = D // NCORE
    HBL = HSL // 128
    VW = NKV * (HD + 1)
    ich = _chunks(IT, ICH)
    maxit = max(ln for _, ln in ich)
    DQQ = cfg["DQQ"]
    kvgroups = [list(range(g * CPB, (g + 1) * CPB))
                for g in range(NCORE // CPB)]
    allg = [list(range(NCORE))]
    scale_qk = float(1.0 / np.sqrt(HD))
    n_lnb = TC // LNB

    nc = bass.Bass("TRN2", target_bir_lowering=False, debug=False)

    xT = nc.dram_tensor("xT", [D, TC], F32, kind="ExternalInput").ap()
    wqT = nc.dram_tensor("wqT", [D, D], F32, kind="ExternalInput").ap()
    wkT = nc.dram_tensor("wkT", [KV, D], F32, kind="ExternalInput").ap()
    wvT = nc.dram_tensor("wvT", [KV, D], F32, kind="ExternalInput").ap()
    woT = nc.dram_tensor("woT", [D, D], F32, kind="ExternalInput").ap()
    gc_ = nc.dram_tensor("gc", [HSL, I], I32, kind="ExternalInput").ap()
    gs_ = nc.dram_tensor("gs", [HSL, I], F32, kind="ExternalInput").ap()
    uc_ = nc.dram_tensor("uc", [HSL, I], I32, kind="ExternalInput").ap()
    us_ = nc.dram_tensor("us", [HSL, I], F32, kind="ExternalInput").ap()
    dc_ = nc.dram_tensor("dc", [I, HSL], I32, kind="ExternalInput").ap()
    ds_ = nc.dram_tensor("ds", [I, HSL], F32, kind="ExternalInput").ap()
    oT = nc.dram_tensor("oT", [D, TC], F32, kind="ExternalOutput").ap()

    qT_d = nc.dram_tensor("qT_d", [D, TC], BF16).ap()
    hoT_d = nc.dram_tensor("hoT_d", [D, TC], BF16).ap()
    part_d = nc.dram_tensor("part_d", [D, TC], F32).ap()
    kT_in = nc.dram_tensor("kT_in", [KV, TC], BF16).ap()
    kT_out = nc.dram_tensor("kT_out", [CPB * KV, TC], BF16).ap()
    v_in = nc.dram_tensor("v_in", [TC, VW], BF16).ap()
    v_out = nc.dram_tensor("v_out", [CPB * TC, VW], BF16).ap()
    qcols_g = [(it0 * 128, itn * 128) for (it0, itn) in _chunks(IT, cfg["DQQ"])]
    wg_in = [nc.dram_tensor("wg_in%d" % qi, [128, qn * HBL], BF16).ap()
             for qi, (q0, qn) in enumerate(qcols_g)]
    wg_out = [nc.dram_tensor("wg_out%d" % qi, [NCORE * 128, qn * HBL],
                             BF16).ap()
              for qi, (q0, qn) in enumerate(qcols_g)]
    wu_in = [nc.dram_tensor("wu_in%d" % qi, [128, qn * HBL], BF16).ap()
             for qi, (q0, qn) in enumerate(qcols_g)]
    wu_out = [nc.dram_tensor("wu_out%d" % qi, [NCORE * 128, qn * HBL],
                             BF16).ap()
              for qi, (q0, qn) in enumerate(qcols_g)]
    wd_in = [nc.dram_tensor("wd_in%d" % ci, [128, citn * HBL * 128],
                            BF16).ap()
             for ci, (cit0, citn) in enumerate(ich)]
    wd_out = [nc.dram_tensor("wd_out%d" % ci, [NCORE * 128,
                                               citn * HBL * 128], BF16).ap()
              for ci, (cit0, citn) in enumerate(ich)]

    from contextlib import ExitStack
    with ExitStack() as stack:
        e_ = stack.enter_context

        _qc = [(it0 * 128, itn * 128) for (it0, itn) in _chunks(IT, DQQ)]
        _gsz = max(1, (max(qn for _, qn in _qc)) // HSL)
        dqq = max(max(qn for _, qn in _qc), _gsz * HSL)
        NQSL = 8
        ySz = max(3 * dqq, 2 * maxit * 128 + 2 * HB * 128)
        zAttn = (2 * STT * 128 + 2 * STT * (HD + 1) + NQSL * TQB +
                 2 * STT * TQB + 4 * 128 + 6 * TQB)
        zProj = 2 * (HB * 128) + 2 * TBW + 2 * VW
        zFfn = maxit * TC + 8 * HBH * 128 + 2 * TBW
        zSz = max(zAttn, zProj, zFfn)
        nbf = HB * TC + ySz + zSz
        A = e_(nc.sbuf_tensor("A", [128, nbf], BF16))
        X0, Y0, Z0 = 0, HB * TC, HB * TC + ySz

        LNE = HB * LNB + 4 * LNB
        fSz = max(LNE + 2 * dqq, 8 * TBW) + 8
        Af = e_(nc.sbuf_tensor("Af", [128, fSz], F32))
        Ai = e_(nc.sbuf_tensor("Ai", [128, 2 * dqq], I32))

        ident = e_(nc.sbuf_tensor("ident", [128, 128], BF16))
        iota_t = e_(nc.sbuf_tensor("iota_t", [128, 128], F32))
        ones_col = e_(nc.sbuf_tensor("ones_col", [128, 1], F32))
        ones_row = e_(nc.sbuf_tensor("ones_row", [1, 128], F32))
        srow = e_(nc.sbuf_tensor("srow", [1, 6 * LNB], F32))
        rloc = e_(nc.sbuf_tensor("rloc", [128, 4], F32))
        P = [e_(nc.psum_tensor(f"ps{i}", [128, 512], F32)) for i in range(7)]
        Pb = e_(nc.psum_tensor("psb", [128, 1024], BF16))

        blk = e_(nc.Block())
        b = B(nc, stack, simsafe=simsafe)
        psfree = {}

        ev_iota = b.op("gpsimd", lambda e: e.iota(
            iota_t[:], pattern=[[1, 128]], base=0, channel_multiplier=-1,
            allow_small_or_imprecise_dtypes=True), inc="gq")
        ev_ident = b.op("vector", lambda e: e.tensor_scalar(
            ident[:], iota_t[:], 0.0, None, ALU.is_equal),
            waits=[ev_iota], inc="vq")
        b.op("vector", lambda e: e.memset(ones_col[:], 1.0), inc="vq")
        ev_ones = b.op("vector", lambda e: e.memset(ones_row[:], 1.0),
                       inc="vq")

        b.phase = "dq"
        cbf = A[:, Y0:Y0 + dqq]
        lut2 = [A[:, Y0 + dqq:Y0 + 2 * dqq],
                A[:, Y0 + 3 * dqq:Y0 + 4 * dqq]]
        dtmp = A[:, Y0 + 2 * dqq:Y0 + 3 * dqq]
        ci32 = [Ai[:, 0:dqq], Ai[:, dqq:2 * dqq]]
        LNE = HB * LNB + 4 * LNB
        scl2 = [Af[:, LNE:LNE + dqq], Af[:, LNE + dqq:LNE + 2 * dqq]]

        def dequant_lut(lut, n, waits):
            ev = b.op("vector", lambda e, n=n, lut=lut: e.tensor_scalar(
                lut[:, :n], cbf[:, :n], 0.0, NF4[0], ALU.is_equal, ALU.mult),
                waits=waits, inc="vq")
            for v in range(1, 16):
                evt = b.op("vector", lambda e, v=v, n=n: e.tensor_scalar(
                    dtmp[:, :n], cbf[:, :n], float(v), NF4[v],
                    ALU.is_equal, ALU.mult), inc="vq")
                ev = b.op("vector", lambda e, n=n, lut=lut: e.tensor_add(
                    lut[:, :n], lut[:, :n], dtmp[:, :n]),
                    waits=[evt], inc="vq")
            return ev

        qcols = [(it0 * 128, itn * 128) for (it0, itn) in _chunks(IT, DQQ)]
        ev_cc_w = {}
        ev_scl_used = [None, None]
        ev_lut_used = None
        ev_cbf_used = None
        ev_ci_used = [None, None]
        dq_i = 0

        ev_lut_used2 = [None, None]
        dq_pend = []

        def _flush_dq_writes(keep=0):
            nonlocal ev_lut_used
            while len(dq_pend) > keep:
                (wf, lslot, ev_m) = dq_pend.pop(0)
                ev_w = wf([ev_m])
                ev_lut_used2[lslot] = ev_w
                ev_lut_used = ev_w
                wf2 = dq_pend_cb.pop(0)
                wf2(ev_w)

        dq_pend_cb = []

        def dq_tile(load_fn, n, write_fn, done_cb):
            nonlocal dq_i, ev_cbf_used
            islot = dq_i % 2
            lslot = dq_i % 2
            lut = lut2[lslot]
            ev_ld32 = load_fn(ci32[islot], "d_cb%d" % islot,
                              [ev_ci_used[islot]])
            sslot2 = dq_i % 2
            scl = scl2[sslot2]
            ev_s = load_fn(scl, "d_sc%d" % sslot2,
                           [ev_scl_used[sslot2]], scales=True)
            dq_i += 1
            ev_cast = b.op("vector", lambda e, n=n, islot=islot:
                           e.tensor_copy(cbf[:, :n], ci32[islot][:, :n]),
                           waits=[ev_ld32, ev_cbf_used], inc="vq")
            ev_ci_used[islot] = ev_cast
            ev_l = dequant_lut(lut, n, [ev_cast, ev_lut_used2[lslot]])
            ev_cbf_used = ev_l
            ev_m = b.op("vector",
                        lambda e, n=n, scl=scl, lut=lut: e.tensor_mul(
                            lut[:, :n], lut[:, :n], scl[:, :n]),
                        waits=[ev_l, ev_s], inc="vq")
            ev_scl_used[sslot2] = ev_m
            dq_pend.append((write_fn, lslot, ev_m))
            dq_pend_cb.append(done_cb)
            _flush_dq_writes(keep=1)

        srcs = {"wg": (gc_, gs_), "wu": (uc_, us_), "wd": (dc_, ds_)}
        wins = {"wg": wg_in, "wu": wu_in, "wd": wd_in}
        wouts = {"wg": wg_out, "wu": wu_out, "wd": wd_out}
        qi_of_it = {}
        for qi, (q0, qn) in enumerate(qcols):
            for it in range(q0 // 128, (q0 + qn) // 128):
                qi_of_it[it] = qi
        gsz2 = max(1, dqq // HSL)
        wd_tile_its = []
        wd_tiles_per_ci = {}
        for ci, (cit0, citn) in enumerate(ich):
            s0w = cit0
            while s0w < cit0 + citn:
                itn = min(gsz2, cit0 + citn - s0w)
                wd_tile_its.append((ci, s0w, itn))
                wd_tiles_per_ci[ci] = wd_tiles_per_ci.get(ci, 0) + 1
                s0w += itn

        dq_worklist = []
        ag_waits = {}

        def _mk_gu_tile(wname, qi, hbl):
            (q0, qn) = qcols[qi]
            itnl = qn // 128
            codes, scarr = srcs[wname]
            win = wins[wname][qi]
            w4 = win.rearrange("p (it hbl i) -> p it hbl i",
                               it=itnl, hbl=HBL)

            def run():
                lslot = dq_i % 2
                lut = lut2[lslot]

                def loadf(dst, sem, waits, scales=False):
                    src = scarr if scales else codes
                    return b.dma(
                        "sync", dst[:, :qn],
                        src[hbl * 128:(hbl + 1) * 128, q0:q0 + qn],
                        inc=sem, waits=waits)

                def writef(waits):
                    return b.dma(
                        "sync", w4[:, :, hbl, :],
                        lut[:, :qn].rearrange("p (it i) -> p it i",
                                              i=128),
                        inc="d_wr", waits=waits)

                def done(ev_w):
                    ag_waits.setdefault((wname, qi), []).append(ev_w)
                    if len(ag_waits[(wname, qi)]) == HBL:
                        ev_cc_w[(wname, qi)] = b.cc(
                            "AllGather", ALU.bypass, allg,
                            [win], [wouts[wname][qi]],
                            inc="cc_%s%d" % (wname, qi),
                            waits=ag_waits[(wname, qi)])
                dq_tile(loadf, qn, writef, done)
            return run

        def _mk_wd_tile(ti):
            (ci, it0, itn) = wd_tile_its[ti]
            (cit0, citn) = ich[ci]
            n = itn * HSL
            codes, scarr = srcs["wd"]
            win = wins["wd"][ci]
            w4 = win.rearrange("p (it hbl i) -> p it hbl i",
                               it=citn, hbl=HBL)
            c3 = codes.rearrange("(it p) h -> p it h", p=128)
            s3 = scarr.rearrange("(it p) h -> p it h", p=128)

            def run():
                lslot = dq_i % 2
                lut = lut2[lslot]

                def loadf(dst, sem, waits, scales=False):
                    src = s3 if scales else c3
                    return b.dma(
                        "sync",
                        dst[:, :n].rearrange("p (it h) -> p it h",
                                             h=HSL),
                        src[:, it0:it0 + itn, :], inc=sem, waits=waits)

                def writef(waits):
                    lut4 = lut[:, :n].rearrange(
                        "p (it hbl i) -> p it hbl i", it=itn, hbl=HBL)
                    evw = None
                    for hbl in range(HBL):
                        evw = b.dma(
                            "sync",
                            w4[:, it0 - cit0:it0 - cit0 + itn, hbl, :],
                            lut4[:, :, hbl, :],
                            inc="d_wr", waits=(waits if hbl == 0 else ()))
                    return evw

                def done(ev_w):
                    ag_waits.setdefault(("wd", ci), []).append(ev_w)
                    if len(ag_waits[("wd", ci)]) == wd_tiles_per_ci[ci]:
                        ev_cc_w[("wd", ci)] = b.cc(
                            "AllGather", ALU.bypass, allg,
                            [win], [wouts["wd"][ci]],
                            inc="cc_wd%d" % ci,
                            waits=ag_waits[("wd", ci)])
                dq_tile(loadf, n, writef, done)
            return run

        for qi in range(len(qcols)):
            for wname in ("wg", "wu"):
                for hbl in range(HBL):
                    dq_worklist.append(_mk_gu_tile(wname, qi, hbl))
        for ti in range(len(wd_tile_its)):
            dq_worklist.append(_mk_wd_tile(ti))
        dq_next = [0]

        def emit_dq_tile(k=1):
            for _ in range(k):
                if dq_next[0] < len(dq_worklist):
                    dq_worklist[dq_next[0]]()
                    dq_next[0] += 1

        b.phase = "ln"
        xnT = A[:, X0:X0 + HB * TC]
        xnT3 = xnT.rearrange("p (hb t) -> p hb t", hb=HB)
        xblk = Af[:, 0:HB * LNB]
        xblk3 = xblk.rearrange("p (hb t) -> p hb t", hb=HB)
        xsq = [Af[:, HB * LNB + i * LNB: HB * LNB + (i + 1) * LNB]
               for i in range(2)]
        rstd_b = Af[:, HB * LNB + 2 * LNB: HB * LNB + 3 * LNB]
        mb_b = Af[:, HB * LNB + 3 * LNB: HB * LNB + 4 * LNB]
        xT3 = xT.rearrange("(hb p) t -> p hb t", p=128)

        ev_xblk_free = None
        ev_xn = None
        ev_xn_w = []
        for tb in range(n_lnb):
            t0 = tb * LNB
            ev_ld = b.dma("sync", xblk3, xT3[:, :, t0:t0 + LNB],
                          inc="d_x", waits=[ev_xblk_free])
            ev_sx = None
            for hb in range(HB):
                w0 = [ev_ld, ev_ones, psfree.get(4)] if hb == 0 else ()
                ev_sx = b.op("tensor", lambda e, hb=hb: e.matmul(
                    P[4][0:1, :LNB], ones_col[:], xblk3[:, hb, :],
                    start=(hb == 0), stop=(hb == HB - 1)),
                    waits=w0, inc="tq")
            ev_sq_mm = []
            for hb in range(HB):
                w0 = [ev_ld]
                if hb >= 2:
                    w0.append(ev_sq_mm[hb - 2])
                if hb < 2:
                    w0.append(psfree.get(5))
                evq = b.op("scalar", lambda e, hb=hb: e.activation(
                    xsq[hb % 2], xblk3[:, hb, :], AF.Square),
                    waits=w0, inc="sq")
                ev_sq_mm.append(b.op("tensor", lambda e, hb=hb: e.matmul(
                    P[5][0:1, :LNB], ones_col[:], xsq[hb % 2],
                    start=(hb == 0), stop=(hb == HB - 1)),
                    waits=[evq], inc="tq"))
            mean = srow[:, 0 * LNB:1 * LNB]
            ex2 = srow[:, 1 * LNB:2 * LNB]
            m2 = srow[:, 2 * LNB:3 * LNB]
            var = srow[:, 3 * LNB:4 * LNB]
            lnv = srow[:, 4 * LNB:5 * LNB]
            rstd = srow[:, 5 * LNB:6 * LNB]
            e1 = b.op("vector", lambda e: e.tensor_scalar(
                mean, P[4][0:1, :LNB], 1.0 / D, None, ALU.mult),
                waits=[ev_sx], inc="vq")
            e2 = b.op("vector", lambda e: e.tensor_scalar(
                ex2, P[5][0:1, :LNB], 1.0 / D, None, ALU.mult),
                waits=[ev_sq_mm[-1]], inc="vq")
            psfree[4], psfree[5] = e1, e2
            e3 = b.op("vector", lambda e: e.tensor_mul(m2, mean, mean),
                      waits=[e1], inc="vq")
            e4a = b.op("vector", lambda e: e.tensor_sub(var, ex2, m2),
                       waits=[e2, e3], inc="vq")
            e4 = b.op("vector", lambda e: e.tensor_scalar(
                var, var, EPS, None, ALU.add), waits=[e4a], inc="vq")
            e5 = b.op("scalar", lambda e: e.activation(
                lnv, var, AF.Ln), waits=[e4], inc="sq")
            e6 = b.op("scalar", lambda e: e.activation(
                rstd, lnv, AF.Exp, scale=-0.5), waits=[e5], inc="sq")
            e7 = b.op("vector", lambda e: e.scalar_tensor_tensor(
                m2, mean, -1.0, rstd, ALU.mult, ALU.mult),
                waits=[e6], inc="vq")
            e8 = b.op("tensor", lambda e: e.matmul(
                P[6][:, 0:LNB], ones_row[:], rstd, start=True, stop=True),
                waits=[e6, psfree.get(6)], inc="tq")
            e9 = b.op("tensor", lambda e: e.matmul(
                P[6][:, LNB:2 * LNB], ones_row[:], m2, start=True, stop=True),
                waits=[e7], inc="tq")
            eb1 = b.op("vector", lambda e: e.tensor_copy(
                rstd_b, P[6][:, 0:LNB]), waits=[e8, e9], inc="vq")
            eb2 = b.op("vector", lambda e: e.tensor_copy(
                mb_b, P[6][:, LNB:2 * LNB]), waits=[e9], inc="vq")
            psfree[6] = eb2
            for hb in range(HB):
                evm = b.op("vector", lambda e, hb=hb: e.tensor_mul(
                    xsq[hb % 2], xblk3[:, hb, :], rstd_b),
                    waits=[eb1, eb2], inc="vq")
                ev_xn = b.op("vector", lambda e, hb=hb, t0=t0: e.tensor_add(
                    xnT3[:, hb, t0:t0 + LNB], xsq[hb % 2], mb_b),
                    waits=[evm], inc="vq")
            ev_xblk_free = ev_xn
            ev_xn_w.append(ev_xn)
            if tb % 2 == 1:
                emit_dq_tile(1)

        b.phase = "proj"
        wpan = [A[:, Z0 + i * HB * 128: Z0 + (i + 1) * HB * 128]
                for i in range(2)]
        ps0 = Z0 + 2 * HB * 128
        pstg = [A[:, ps0 + i * TBW: ps0 + (i + 1) * TBW] for i in range(2)]
        vs0 = ps0 + 2 * TBW
        vstg = [A[:, vs0 + i * VW: vs0 + (i + 1) * VW] for i in range(2)]

        ev_vst_used = []
        for i in range(2):
            ev_vst_used.append(b.op(
                "vector", lambda e, i=i: e.memset(
                    vstg[i].rearrange("p (h c) -> p h c", c=HD + 1)[:, :, HD:],
                    1.0), inc="vq"))

        def cast_panel(slot, wT, dt, waits, sem):
            return b.dma("gpsimd", wpan[slot][:, :HB * 128],
                         wT[dt * 128:(dt + 1) * 128, :],
                         inc=sem, waits=waits)

        ev_pan_used = [None, None]
        ev_pstg_used = [None, None]
        kv_evs = []
        q_evs = []
        pidx = 0

        WLN = TBW // LNB
        for dt in range(KV // 128):
            slot = pidx % 2
            ev_pan = cast_panel(slot, wkT, dt, [ev_pan_used[slot]],
                                "d_wp%d" % slot)
            ev_mm = None
            for tb in range(NTB):
                bank = (pidx * NTB + tb) % 4
                for hb in range(HB):
                    w0 = ([ev_pan, ev_xn_w[(tb + 1) * WLN - 1],
                           psfree.get(bank)]
                          if hb == 0 else ())
                    ev_mm = b.op("tensor", lambda e, hb=hb, tb=tb, bank=bank, slot=slot: e.matmul(
                        P[bank][:, :TBW],
                        wpan[slot].rearrange("p (hb d) -> p hb d", hb=HB)[:, hb, :128],
                        xnT3[:, hb, tb * TBW:(tb + 1) * TBW],
                        start=(hb == 0), stop=(hb == HB - 1)),
                        waits=w0, inc="tq")
                sslot = bank % 2
                ev_ev = b.op("scalar", lambda e, bank=bank, sslot=sslot: e.activation(
                    pstg[sslot][:, :TBW], P[bank][:, :TBW], AF.Copy),
                    waits=[ev_mm, ev_pstg_used[sslot]], inc="sq")
                psfree[bank] = ev_ev
                ev_st = b.dma("scalar",
                              kT_in[dt * 128:(dt + 1) * 128,
                                    tb * TBW:(tb + 1) * TBW],
                              pstg[sslot][:, :TBW], inc="d_ps%d" % sslot,
                              waits=[ev_ev])
                ev_pstg_used[sslot] = ev_st
                kv_evs.append(ev_st)
            ev_pan_used[slot] = ev_mm
            pidx += 1
        nwv = 128
        for dh in range(KV // nwv):
            slot = pidx % 2
            ev_pan = cast_panel(slot, wvT, dh, [ev_pan_used[slot]],
                                "d_wp%d" % slot)
            ev_mm = None
            for tt in range(TC // 128):
                bank = (pidx * (TC // 128) + tt) % 4
                for hb in range(HB):
                    w0 = ([ev_pan, ev_xn_w[(tt + 1) * (128 // LNB) - 1],
                           psfree.get(bank)]
                          if hb == 0 else ())
                    ev_mm = b.op("tensor", lambda e, hb=hb, tt=tt, bank=bank, slot=slot, nwv=nwv: e.matmul(
                        P[bank][:, :nwv],
                        xnT3[:, hb, tt * 128:(tt + 1) * 128],
                        wpan[slot].rearrange("p (hb d) -> p hb d", hb=HB)[:, hb, :nwv],
                        start=(hb == 0), stop=(hb == HB - 1)),
                        waits=w0, inc="tq")
                vslot = (pidx * (TC // 128) + tt) % 2
                ev_ev = b.op("scalar", lambda e, bank=bank, vslot=vslot: e.activation(
                    vstg[vslot].rearrange("p (h c) -> p h c",
                                          c=HD + 1)[:, 0, :HD],
                    P[bank][:, 0:HD], AF.Copy),
                    waits=[ev_mm, ev_vst_used[vslot]], inc="sq")
                psfree[bank] = ev_ev
                ev_st = b.dma(
                    "scalar",
                    v_in[tt * 128:(tt + 1) * 128,
                         dh * (HD + 1):(dh + 1) * (HD + 1)],
                    vstg[vslot][:, :HD + 1],
                    inc="d_vs%d" % vslot, waits=[ev_ev])
                ev_vst_used[vslot] = ev_st
                kv_evs.append(ev_st)
            ev_pan_used[slot] = ev_mm
            pidx += 1
        ev_cc_k = b.cc("AllGather", ALU.bypass, kvgroups, [kT_in], [kT_out],
                       inc="cck", waits=kv_evs)
        ev_cc_v = b.cc("AllGather", ALU.bypass, kvgroups, [v_in], [v_out],
                       inc="ccv", waits=kv_evs)
        for dt in range(D // 128):
            slot = pidx % 2
            ev_pan = cast_panel(slot, wqT, dt, [ev_pan_used[slot]],
                                "d_wp%d" % slot)
            ev_mm = None
            for tb in range(NTB):
                bank = (pidx * NTB + tb) % 4
                for hb in range(HB):
                    w0 = ([ev_pan, ev_xn_w[(tb + 1) * WLN - 1],
                           psfree.get(bank)]
                          if hb == 0 else ())
                    ev_mm = b.op("tensor", lambda e, hb=hb, tb=tb, bank=bank, slot=slot: e.matmul(
                        P[bank][:, :TBW],
                        wpan[slot].rearrange("p (hb d) -> p hb d", hb=HB)[:, hb, :128],
                        xnT3[:, hb, tb * TBW:(tb + 1) * TBW],
                        start=(hb == 0), stop=(hb == HB - 1)),
                        waits=w0, inc="tq")
                sslot = bank % 2
                ev_ev = b.op("scalar", lambda e, bank=bank, sslot=sslot: e.activation(
                    pstg[sslot][:, :TBW], P[bank][:, :TBW], AF.Copy),
                    waits=[ev_mm, ev_pstg_used[sslot]], inc="sq")
                psfree[bank] = ev_ev
                ev_st = b.dma("scalar",
                              qT_d[dt * 128:(dt + 1) * 128,
                                   tb * TBW:(tb + 1) * TBW],
                              pstg[sslot][:, :TBW], inc="d_ps%d" % sslot,
                              waits=[ev_ev])
                ev_pstg_used[sslot] = ev_st
                q_evs.append(ev_st)
            ev_pan_used[slot] = ev_mm
            pidx += 1

        emit_dq_tile(len(dq_worklist))
        _flush_dq_writes(0)
        ev_dq_done = ev_lut_used

        b.phase = "attn"
        za = Z0
        kTs = [A[:, za + i * STT * 128: za + (i + 1) * STT * 128]
               for i in range(2)]
        za += 2 * STT * 128
        Vs = [A[:, za + i * STT * (HD + 1): za + (i + 1) * STT * (HD + 1)]
              for i in range(2)]
        za += 2 * STT * (HD + 1)
        qTs = [A[:, za + i * TQB: za + (i + 1) * TQB] for i in range(NQSL)]
        za += NQSL * TQB
        expP = [A[:, za + i * STT * TQB: za + (i + 1) * STT * TQB]
                for i in range(2)]
        za += 2 * STT * TQB
        hosc = [A[:, za + i * 128: za + (i + 1) * 128] for i in range(4)]
        za += 4 * 128
        hostg = [A[:, za + i * TQB: za + (i + 1) * TQB] for i in range(6)]

        kT_o3 = kT_out.rearrange("(r d) t -> d r t", r=CPB)
        v_o3 = v_out.rearrange("(tt p) c -> p tt c", p=128)

        ev_att_by_u = []
        ev_sc_last = {}
        ev_q_used = [None] * NQSL
        ev_hosc_used = [None, None, None, None]
        ev_hostg_used = [None] * 6
        ev_trev_prev = [None, None]
        ho_evs = []
        pend = None
        u = 0

        def emit_epilogue(st, ev_avs):
            nonlocal ev_trev_prev
            (qh, tqb, es, ks, ev_exp_u, ev_lv, uu) = st
            avbs = [4, 5] if uu % 2 == 0 else [3, 6]
            ev_tr = None
            els = []
            ers = []
            for tqs in range(NQS):
                avb = avbs[tqs % 2]
                els.append(b.op("scalar", lambda e, tqs=tqs, avb=avb:
                                e.activation(
                                    rloc[:, tqs:tqs + 1],
                                    P[avb][:, HD:HD + 1],
                                    AF.Ln), waits=[ev_avs[tqs]], inc="sq"))
            for tqs in range(NQS):
                ers.append(b.op("scalar", lambda e, tqs=tqs: e.activation(
                    rloc[:, 2 + tqs:3 + tqs], rloc[:, tqs:tqs + 1],
                    AF.Exp, scale=-1.0), waits=[els[tqs]], inc="sq"))
            for tqs in range(NQS):
                avb = avbs[tqs % 2]
                hs = (uu % 2) * 2 + (tqs % 2)
                eh = b.op("scalar", lambda e, tqs=tqs, avb=avb, hs=hs: e.activation(
                    hosc[hs], P[avb][:, :HD], AF.Copy,
                    scale=rloc[:, 2 + tqs:3 + tqs]),
                    waits=[ers[tqs], ev_hosc_used[hs]], inc="sq")
                psfree[avb] = eh
                pbh = (uu % 2) * NQS * 128
                ev_tr = b.op("tensor", lambda e, tqs=tqs, hs=hs, pbh=pbh: e.transpose(
                    Pb[:, pbh + tqs * 128:pbh + (tqs + 1) * 128], hosc[hs],
                    ident[:]),
                    waits=[eh, ev_ident, ev_trev_prev[uu % 2]], inc="tq")
                ev_hosc_used[hs] = ev_tr
            ev_att_by_u.append(ev_avs[-1])
            hss = len(ev_att_by_u) % 6
            pbh = (uu % 2) * NQS * 128
            ev_he = b.op("scalar", lambda e, hss=hss, pbh=pbh: e.activation(
                hostg[hss][:, :TQB], Pb[:, pbh:pbh + TQB], AF.Copy),
                waits=[ev_tr, ev_hostg_used[hss]], inc="sq")
            ev_trev_prev[uu % 2] = ev_he
            ev_hst = b.dma(
                "scalar",
                hoT_d[qh * 128:(qh + 1) * 128,
                      tqb * TQB:(tqb + 1) * TQB],
                hostg[hss][:, :TQB], inc="d_h%d" % hss,
                waits=[ev_he])
            ev_hostg_used[hss] = ev_hst
            ho_evs.append(ev_hst)

        def emit_av_pair(st, pair, ev_avs):
            (qh, tqb, es, ks, ev_exp_u, ev_lv, uu) = st
            avbs = [4, 5] if uu % 2 == 0 else [3, 6]
            for tqs in range(NQS):
                avb = avbs[tqs % 2]
                for tkt in (2 * pair, 2 * pair + 1):
                    w0 = []
                    if tqs == 0:
                        w0.append(ev_exp_u[pair])
                    if tkt == 0:
                        w0 += [ev_lv, psfree.get(avb)]
                    ev_avs[tqs] = b.op("tensor", lambda e, tkt=tkt, tqs=tqs, avb=avb, es=es, ks=ks: e.matmul(
                        P[avb][:, :HD + 1],
                        expP[es][:, tkt * TQB + tqs * 128:
                                 tkt * TQB + tqs * 128 + 128],
                        Vs[ks].rearrange("p (tt c) -> p tt c",
                                         c=HD + 1)[:, tkt, :],
                        start=(tkt == 0), stop=(tkt == STT - 1),
                        skip_group_check=True),
                        waits=w0, inc="tq")

        units = [(kvh, g, tqb, kvh * GQ + g)
                 for kvh in range(NKV) for g in range(GQ)
                 for tqb in range(NTQB)]

        ev_lk = {}
        ev_lv = {}
        ev_lq_by_u = {}

        def load_kv(kvh):
            ks = kvh % 2
            ev_lk[kvh] = b.dma(
                "scalar", kTs[ks].rearrange("p (r t) -> p r t", r=CPB),
                kT_o3[kvh * 128:(kvh + 1) * 128, :, :],
                inc="d_k%d" % ks,
                waits=[ev_cc_k, ev_sc_last.get(kvh - 2)] + q_evs)
            wv_v = [ev_cc_v] + q_evs
            if kvh >= 2:
                wv_v.append(ev_att_by_u[(kvh - 1) * GQ * NTQB - 1])
            ev_lv[kvh] = b.dma(
                "scalar", Vs[ks].rearrange("p (tt c) -> p tt c", c=HD + 1),
                v_o3[:, :, kvh * (HD + 1):(kvh + 1) * (HD + 1)],
                inc="d_v%d" % ks, waits=wv_v)

        def load_q(uu):
            qh = units[uu][3]
            tqb = units[uu][2]
            qs = uu % NQSL
            ev_lq_by_u[uu] = b.dma(
                "scalar", qTs[qs],
                qT_d[qh * 128:(qh + 1) * 128, tqb * TQB:(tqb + 1) * TQB],
                inc="d_q%d" % qs,
                waits=[ev_q_used[qs]] + q_evs)

        load_kv(0)
        for w in range(min(NQSL, len(units))):
            load_q(w)
        for u, (kvh, g, tqb, qh) in enumerate(units):
            ks = kvh % 2
            qs = u % NQSL
            es = u % 2
            if g == 0 and tqb == min(1, NTQB - 1) and kvh + 1 < NKV:
                load_kv(kvh + 1)
            ev_exp_u = []
            ev_avs = [None] * NQS
            ev_mm = None
            for tkt in range(STT):
                pair, half = tkt // 2, tkt % 2
                bank = pair % 3
                w0 = [ev_lk[kvh], ev_lq_by_u[u]]
                if half == 0:
                    w0.append(psfree.get(bank))
                ev_mm = b.op("tensor", lambda e, tkt=tkt, bank=bank, half=half, ks=ks, qs=qs: e.matmul(
                    P[bank][:, half * TQB:(half + 1) * TQB],
                    kTs[ks][:, tkt * 128:(tkt + 1) * 128],
                    qTs[qs][:, :TQB], start=True, stop=True),
                    waits=w0, inc="tq")
                ev_sc_last[kvh] = ev_mm
                if half == 1:
                    eex = b.op("scalar", lambda e, pair=pair, bank=bank, es=es: e.activation(
                        expP[es][:, 2 * pair * TQB:2 * (pair + 1) * TQB],
                        P[bank][:, :2 * TQB], AF.Exp, scale=scale_qk),
                        waits=[ev_mm], inc="sq")
                    ev_exp_u.append(eex)
                    psfree[bank] = eex
                    if pend is not None:
                        emit_av_pair(pend[0], pair, pend[1])
            ev_q_used[qs] = ev_mm
            if u + NQSL < len(units):
                load_q(u + NQSL)
            if pend is not None:
                emit_epilogue(pend[0], pend[1])
            pend = ((qh, tqb, es, ks, ev_exp_u, ev_lv[kvh], u), ev_avs)
        for pair in range(STT // 2):
            emit_av_pair(pend[0], pair, pend[1])
        emit_epilogue(pend[0], pend[1])

        b.phase = "ffn"
        zf = Z0
        intT = A[:, zf:zf + maxit * TC]
        zf += maxit * TC
        gpan = [A[:, zf + i * HBH * 128: zf + (i + 1) * HBH * 128]
                for i in range(8)]
        zf += 8 * HBH * 128
        sg = [A[:, zf + i * TBW: zf + (i + 1) * TBW] for i in range(2)]
        wdp = [A[:, Y0 + i * maxit * 128: Y0 + (i + 1) * maxit * 128]
               for i in range(2)]
        wop = [A[:, Y0 + 2 * maxit * 128 + i * HB * 128:
                 Y0 + 2 * maxit * 128 + (i + 1) * HB * 128]
               for i in range(2)]
        hoT_sb = A[:, X0:X0 + HB * TC]
        hoT_sb3 = hoT_sb.rearrange("p (hb t) -> p hb t", hb=HB)
        pastg = [Af[:, i * TBW:(i + 1) * TBW] for i in range(2)]
        outstg = [Af[:, (2 + i) * TBW:(3 + i) * TBW] for i in range(2)]
        xr = [Af[:, (4 + i) * TBW:(5 + i) * TBW] for i in range(4)]
        upstg = [Af[:, (8 + i) * TBW:(9 + i) * TBW] for i in range(2)]

        wg_o = [w.rearrange("(r p) (it q) -> p it r q", p=128,
                            it=qn // 128)
                for w, (q0, qn) in zip(wg_out, qcols)]
        wu_o = [w.rearrange("(r p) (it q) -> p it r q", p=128,
                            it=qn // 128)
                for w, (q0, qn) in zip(wu_out, qcols)]
        wd_o = [w.rearrange("(r p) (it hbl i) -> p r it hbl i",
                            p=128, it=citn, hbl=HBL)
                for w, (cit0, citn) in zip(wd_out, ich)]



        ev_gpan_used = [None] * 8
        ev_sg_used = [None, None]
        ev_ups_used = [ev_scl_used[0], ev_scl_used[1]]
        ev_int_rd = None
        ev_wdp_used = [None, None]
        ev_wop_used = [None, None]
        ev_stg_used = [None, None]
        ev_xr_used = [None] * 4
        ev_xr_ld = {}
        ev_pa_st = {}
        out_evs = []
        ffn_first = [ev_xn]
        dn_pair = [(4, 5), (0, 1)]

        def load_gu(kind, it, itl):
            evs = []
            qi = qi_of_it[it]
            itloc = it - qcols[qi][0] // 128
            src = (wg_o if kind == 0 else wu_o)[qi]
            ccw = ev_cc_w[("wg" if kind == 0 else "wu", qi)]
            for half in range(2):
                slot = (itl % 2) * 4 + kind * 2 + half
                rh = NCORE // 2
                ev = b.dma(
                    "scalar",
                    gpan[slot].rearrange("p (r q) -> p r q", r=rh),
                    src[:, itloc, half * rh:(half + 1) * rh, :],
                    inc="d_gp%d" % slot,
                    waits=[ccw, ev_gpan_used[slot]] + ho_evs)
                evs.append(ev)
            return evs

        def load_dn(ci, ht, cit0, citn, last_chunk):
            wslot = ht % 2
            ev_wd = b.dma("sync",
                          wdp[wslot][:, :citn * 128].rearrange(
                              "p (it i) -> p it i", i=128),
                          wd_o[ci][:, ht // HBL, :, ht % HBL, :],
                          inc="d_wdl%d" % wslot,
                          waits=[ev_cc_w[("wd", ci)], ev_dq_done,
                                 ev_wdp_used[wslot]])
            ev_wo = None
            if last_chunk:
                ev_wo = b.dma("gpsimd",
                              wop[wslot][:, :HB * 128],
                              woT[ht * 128:(ht + 1) * 128, :],
                              inc="d_wo%d" % wslot,
                              waits=[ev_wop_used[wslot], ev_dq_done])
            for tb in range(NTB):
                xslot = (ht % 2) * 2 + tb
                if ci == 0:
                    src = xT[ht * 128:(ht + 1) * 128,
                             tb * TBW:(tb + 1) * TBW]
                    w0 = [ev_xr_used[xslot], ev_dq_done]
                else:
                    src = part_d[ht * 128:(ht + 1) * 128,
                                 tb * TBW:(tb + 1) * TBW]
                    w0 = [ev_xr_used[xslot]] + ev_pa_st[(ci - 1, ht)]
                ev_xr_ld[(ht, tb)] = b.dma("sync", xr[xslot], src,
                                           inc="d_xr%d" % xslot, waits=w0)
            return ev_wd, ev_wo

        for ci, (cit0, citn) in enumerate(ich):
            last_chunk = (ci == ICH - 1)
            ev_dve_int = None
            gate_mm = [None] * NTB
            up_mm = [None] * NTB
            for itl in range(citn):
                it = cit0 + itl
                pg = load_gu(0, it, itl)
                pu = load_gu(1, it, itl)
                for kind in range(2):
                    pan = pg if kind == 0 else pu
                    for tb in range(NTB):
                        bank = kind * 2 + (tb % 2)
                        ev_mm = None
                        for hb in range(HB):
                            half, hloc = divmod(hb, HBH)
                            slot = (itl % 2) * 4 + kind * 2 + half
                            w0 = ()
                            if hb == 0:
                                w0 = [pan[0], psfree.get(bank)] + ffn_first
                                if itl == 0 and tb == 0 and kind == 0:
                                    w0 = w0 + [ev_int_rd]
                            elif hb == HBH:
                                w0 = [pan[1]]
                            ev_mm = b.op(
                                "tensor",
                                lambda e, hb=hb, hloc=hloc, slot=slot,
                                bank=bank, tb=tb: e.matmul(
                                    P[bank][:, :TBW],
                                    gpan[slot].rearrange(
                                        "p (hb i) -> p hb i",
                                        hb=HBH)[:, hloc, :],
                                    xnT3[:, hb, tb * TBW:(tb + 1) * TBW],
                                    start=(hb == 0), stop=(hb == HB - 1)),
                                waits=w0, inc="tq")
                        ffn_first = []
                        if kind == 0:
                            gate_mm[tb] = ev_mm
                        else:
                            up_mm[tb] = ev_mm
                for j in range(4):
                    ev_gpan_used[(itl % 2) * 4 + j] = up_mm[NTB - 1]
                for tb in range(NTB):
                    sslot = (itl * NTB + tb) % 2
                    ev_si = b.op("scalar", lambda e, tb=tb, sslot=sslot:
                                 e.activation(sg[sslot][:, :TBW],
                                              P[tb % 2][:, :TBW], AF.Silu),
                                 waits=[gate_mm[tb], ev_sg_used[sslot]],
                                 inc="sq")
                    psfree[tb % 2] = ev_si
                    ev_up = b.op("scalar", lambda e, tb=tb, sslot=sslot:
                                 e.activation(upstg[sslot][:, :TBW],
                                              P[2 + (tb % 2)][:, :TBW],
                                              AF.Copy),
                                 waits=[up_mm[tb], ev_ups_used[sslot]],
                                 inc="sq")
                    psfree[2 + (tb % 2)] = ev_up
                    ev_dve_int = b.op(
                        "gpsimd", lambda e, itl=itl, tb=tb, sslot=sslot:
                        e.tensor_mul(
                            intT[:, itl * TC + tb * TBW:
                                 itl * TC + (tb + 1) * TBW],
                            sg[sslot][:, :TBW], upstg[sslot][:, :TBW]),
                        waits=[ev_si, ev_up], inc="gq")
                    ev_sg_used[sslot] = ev_dve_int
                    ev_ups_used[sslot] = ev_dve_int
            ev_gu_done = up_mm[NTB - 1]
            ev_hoT_ld = None
            if last_chunk:
                ev_hoT_ld = b.dma("sync", hoT_sb3,
                                  hoT_d.rearrange("(hb p) t -> p hb t",
                                                  p=128),
                                  inc="d_hol",
                                  waits=[ev_gu_done] + ho_evs)
            ev_wd, ev_wo = load_dn(ci, 0, cit0, citn, last_chunk)
            for ht in range(HB):
                wslot = ht % 2
                ev_wd_cur, ev_wo_cur = ev_wd, ev_wo
                if ht + 1 < HB:
                    ev_wd, ev_wo = load_dn(ci, ht + 1, cit0, citn,
                                           last_chunk)
                for tb in range(NTB):
                    bank = dn_pair[ht % 2][tb]
                    ev_mm = None
                    for itl in range(citn):
                        w0 = ()
                        if itl == 0:
                            w0 = [ev_wd_cur, psfree.get(bank), ev_dve_int]
                        ev_mm = b.op(
                            "tensor",
                            lambda e, itl=itl, tb=tb, bank=bank, wslot=wslot,
                            citn=citn, last_chunk=last_chunk: e.matmul(
                                P[bank][:, :TBW],
                                wdp[wslot][:, itl * 128:(itl + 1) * 128],
                                intT[:, itl * TC + tb * TBW:
                                     itl * TC + (tb + 1) * TBW],
                                start=(itl == 0),
                                stop=(itl == citn - 1 and not last_chunk)),
                            waits=w0, inc="tq")
                    if last_chunk:
                        for hb in range(HB):
                            w0 = [ev_wo_cur, ev_hoT_ld] if hb == 0 else ()
                            ev_mm = b.op(
                                "tensor",
                                lambda e, hb=hb, tb=tb, bank=bank,
                                wslot=wslot: e.matmul(
                                    P[bank][:, :TBW],
                                    wop[wslot].rearrange(
                                        "p (hb i) -> p hb i",
                                        hb=HB)[:, hb, :],
                                    hoT_sb3[:, hb,
                                            tb * TBW:(tb + 1) * TBW],
                                    start=False, stop=(hb == HB - 1)),
                                waits=w0, inc="tq")
                    ev_int_rd = ev_mm
                    ev_wdp_used[wslot] = ev_mm
                    if last_chunk:
                        ev_wop_used[wslot] = ev_mm
                    stg = tb % 2
                    xslot = (ht % 2) * 2 + tb
                    dststg = outstg[stg] if last_chunk else pastg[stg]
                    ev_e = b.op(
                        "vector",
                        lambda e, bank=bank, dststg=dststg, xslot=xslot:
                        e.tensor_add(dststg, P[bank][:, :TBW], xr[xslot]),
                        waits=[ev_mm, ev_xr_ld[(ht, tb)],
                               ev_stg_used[stg]], inc="vq")
                    psfree[bank] = ev_e
                    ev_xr_used[xslot] = ev_e
                    dst = (oT if last_chunk else part_d)[
                        ht * 128:(ht + 1) * 128, tb * TBW:(tb + 1) * TBW]
                    ev_s = b.dma("sync", dst, dststg,
                                 inc=("d_out%d" % stg if last_chunk
                                      else "d_pa%d" % stg),
                                 waits=[ev_e])
                    ev_stg_used[stg] = ev_s
                    if last_chunk:
                        out_evs.append(ev_s)
                    else:
                        ev_pa_st.setdefault((ci, ht), []).append(ev_s)

        b.wait_only("sync", out_evs)
        b.emit(blk)
    return nc


def _prep_inputs(x, Wq, Wk, Wv, Wo, gcodes, gscales, ucodes, uscales,
                 dcodes, dscales, cfg):
    D, TC, NCORE = cfg["D"], cfg["TC"], cfg["NCORE"]
    HSL = D // NCORE
    BLK = 64
    S_all = x.reshape(-1, D)
    xTf = np.ascontiguousarray(S_all.T.astype(np.float32))

    def _pl(wT):
        Kd, Md = wT.shape
        return np.ascontiguousarray(
            wT.reshape(Kd // 128, 128, Md // 128, 128)
            .transpose(2, 1, 0, 3).reshape(Md, Kd))
    wqT = _pl(Wq.T)
    wkT = _pl(Wk.T)
    wvT = _pl(Wv.T)
    woT = _pl(Wo.T)
    gcT = np.ascontiguousarray(gcodes.T)
    ucT = np.ascontiguousarray(ucodes.T)
    gsx = np.repeat(np.ascontiguousarray(gscales.T), BLK, axis=0)
    usx = np.repeat(np.ascontiguousarray(uscales.T), BLK, axis=0)
    dcT = np.ascontiguousarray(dcodes.T)
    maps = []
    for c in range(NCORE):
        dsx_c = np.repeat(
            np.ascontiguousarray(dscales.T[:, c * HSL:(c + 1) * HSL]),
            BLK, axis=0)
        maps.append({
            "xT": np.ascontiguousarray(xTf[:, c * TC:(c + 1) * TC]),
            "wqT": wqT, "wkT": wkT, "wvT": wvT, "woT": woT,
            "gc": np.ascontiguousarray(gcT[c * HSL:(c + 1) * HSL]),
            "gs": np.ascontiguousarray(gsx[c * HSL:(c + 1) * HSL]),
            "uc": np.ascontiguousarray(ucT[c * HSL:(c + 1) * HSL]),
            "us": np.ascontiguousarray(usx[c * HSL:(c + 1) * HSL]),
            "dc": np.ascontiguousarray(dcT[:, c * HSL:(c + 1) * HSL]),
            "ds": np.ascontiguousarray(dsx_c),
        })
    return maps


def run_cfg(inputs_dict, cfg, simsafe=False):
    maps = _prep_inputs(cfg=cfg, **inputs_dict)
    nc = build_program(cfg, simsafe=simsafe)
    res = run_bass_kernel_spmd(nc, maps, core_ids=list(range(cfg["NCORE"])))
    outs = [np.asarray(r["oT"], np.float32) for r in res.results]
    full = np.concatenate(outs, axis=1)
    return np.ascontiguousarray(full.T)


def kernel(L0_attn_x, L0_attn_Wq, L0_attn_Wk, L0_attn_Wv, L0_attn_Wo,
           L0_ffn_gate_codes, L0_ffn_gate_scales, L0_ffn_up_codes,
           L0_ffn_up_scales, L0_ffn_down_codes, L0_ffn_down_scales):
    cfg = FULL_CFG
    x = np.asarray(L0_attn_x, np.float32)
    B_, T_, D_ = x.shape
    out = run_cfg(cfg=cfg, simsafe=False, inputs_dict=dict(
        x=x,
        Wq=np.asarray(L0_attn_Wq, np.float32),
        Wk=np.asarray(L0_attn_Wk, np.float32),
        Wv=np.asarray(L0_attn_Wv, np.float32),
        Wo=np.asarray(L0_attn_Wo, np.float32),
        gcodes=np.asarray(L0_ffn_gate_codes, np.int32),
        gscales=np.asarray(L0_ffn_gate_scales, np.float32),
        ucodes=np.asarray(L0_ffn_up_codes, np.int32),
        uscales=np.asarray(L0_ffn_up_scales, np.float32),
        dcodes=np.asarray(L0_ffn_down_codes, np.int32),
        dscales=np.asarray(L0_ffn_down_scales, np.float32),
    ))
    return out.reshape(B_, T_, D_)
